# version 02 (final)
# baseline (speedup 1.0000x reference)
import numpy as np
import ml_dtypes
from contextlib import ExitStack
import concourse.bass as bass
import concourse.mybir as mybir
from concourse.bass_utils import run_bass_kernel_spmd
from concourse.alu_op_type import AluOpType as OP
from concourse.bass import IndirectOffsetOnAxis

F32 = mybir.dt.float32
BF16 = mybir.dt.bfloat16
U32 = mybir.dt.uint32
U16 = mybir.dt.uint16
I16 = mybir.dt.int16
AF = mybir.ActivationFunctionType
BF = ml_dtypes.bfloat16
AX = mybir.AxisListType

NC = 8
TL = 2048
D = 1024
NB = TL // 512
T = 8192


def core_chunks(c):
    b, q = divmod(c, 4)
    return b, (q, 7 - q)


class K:
    def __init__(self):
        self.nc = bass.Bass("TRN2", target_bir_lowering=False)
        self.st = ExitStack()
        self.S = {}
        self.c = {}

    def sem(self, *names):
        for n in names:
            if n not in self.S:
                self.S[n] = self.st.enter_context(self.nc.semaphore(n))
                self.c[n] = 0
        return [self.S[n] for n in names]

    def sb(self, name, shape, dtype=F32):
        return self.st.enter_context(self.nc.sbuf_tensor(name, shape, dtype))

    def ps(self, name, shape, dtype=F32):
        return self.st.enter_context(self.nc.psum_tensor(name, shape, dtype))

    def din(self, name, shape, dtype=F32):
        return self.nc.dram_tensor(name, shape, dtype, kind="ExternalInput").ap()

    def dout(self, name, shape, dtype=F32):
        return self.nc.dram_tensor(name, shape, dtype, kind="ExternalOutput").ap()

    def dtmp(self, name, shape, dtype=F32):
        return self.nc.dram_tensor(name, shape, dtype)

    def inc(self, inst, sem, n=1):
        inst.then_inc(self.S[sem], n)
        self.c[sem] += n
        return self.c[sem]

    def w(self, eng, sem, val=None):
        eng.wait_ge(self.S[sem], self.c[sem] if val is None else val)

    def sig(self, eng, sem, n=1):
        eng.maybe_drain_then_inc((self.S[sem], n))
        self.c[sem] += n
        return self.c[sem]

    def bar(self, eng, sem):
        self.sig(eng, sem)
        self.w(eng, sem)

    def dma(self, eng, out, in_, sem="ld", **kw):
        return self.inc(eng.dma_start(out, in_, **kw), sem, 16)

    def close(self):
        self.st.close()


def emit_ident(k):
    nc = k.nc
    k.sem("cst")
    iv = k.sb("c_iv", [128, 128], F32)
    ident = k.sb("c_id", [128, 128], BF16)
    nc.gpsimd.iota(iv[:], [[1, 128]], base=0, channel_multiplier=-1,
                   allow_small_or_imprecise_dtypes=True)
    k.sig(nc.gpsimd, "cst")
    k.w(nc.vector, "cst")
    nc.vector.tensor_scalar(ident[:], iv[:], 0.0, None, OP.is_equal)
    k.sig(nc.vector, "cst")
    return ident


def emit_diag_masks(k):
    nc = k.nc
    k.sem("cst")
    mv = k.sb("c_mv", [128, 512], F32)
    msk = k.sb("c_msk", [128, 4 * 512], BF16)
    nc.gpsimd.iota(mv[:], [[1, 512]], base=0, channel_multiplier=-1,
                   allow_small_or_imprecise_dtypes=True)
    k.sig(nc.gpsimd, "cst")
    k.w(nc.vector, "cst")
    for j, dd in enumerate((0, 128, 256, 384)):
        nc.vector.tensor_scalar(msk[:, j * 512:(j + 1) * 512], mv[:], float(dd), None, OP.is_ge)
    k.sig(nc.vector, "cst")
    return msk


def emit_ln(k, x_sb, xn_sb, scr, TLc=TL, eps=1e-5, xoffs=None, xnoffs=None, blocks=None):
    nc = k.nc
    t, v, s = nc.tensor, nc.vector, nc.scalar
    if blocks is None:
        blocks = [(j * 512, 512) for j in range(TLc // 512)]
    if xoffs is None:
        xoffs = [0] * 8
    if xnoffs is None:
        xnoffs = [0] * 8

    def xs(ci, sl):
        return x_sb[ci][:, sl.start + xoffs[ci]:sl.stop + xoffs[ci]]

    def xns(ci, sl):
        return xn_sb[ci][:, sl.start + xnoffs[ci]:sl.stop + xnoffs[ci]]
    sq, ps_s, ps_q = scr["sq"], scr["ps_s"], scr["ps_q"]
    ps_b1, ps_b2 = scr["ps_b1"], scr["ps_b2"]
    m_row, r_row, t1 = scr["m_row"], scr["r_row"], scr["t1"]
    ones128, ones1 = scr["ones128"], scr["ones1"]
    k.sem("lnv", "lnt", "lns")
    for (boff, bw) in blocks:
        sl = slice(boff, boff + bw)
        for ci in range(8):
            sqb = sq[:, (ci % 2) * 512:(ci % 2) * 512 + bw]
            if ci >= 2:
                k.w(v, "lnt")
            v.tensor_tensor(sqb, xs(ci, sl), xs(ci, sl), OP.mult)
            k.sig(v, "lnv")
            k.w(t, "lnv")
            t.matmul(ps_s[0:1, 0:bw], ones128[:], xs(ci, sl), start=(ci == 0), stop=(ci == 7))
            k.inc(t.matmul(ps_q[0:1, 0:bw], ones128[:], sqb, start=(ci == 0), stop=(ci == 7)), "lnt")
        k.w(v, "lnt")
        v.tensor_scalar(m_row[:, 0:bw], ps_s[0:1, 0:bw], 1.0 / D, None, OP.mult)
        k.sig(v, "lnv")
        v.tensor_scalar(r_row[:, 0:bw], ps_q[0:1, 0:bw], 1.0 / D, None, OP.mult)
        k.w(v, "lnv")
        v.tensor_tensor(sq[0:1, 0:bw], m_row[:, 0:bw], m_row[:, 0:bw], OP.mult)
        k.sig(v, "lnv")
        k.w(v, "lnv")
        v.tensor_tensor(r_row[:, 0:bw], r_row[:, 0:bw], sq[0:1, 0:bw], OP.subtract)
        k.sig(v, "lnv")
        k.w(v, "lnv")
        v.tensor_scalar(r_row[:, 0:bw], r_row[:, 0:bw], float(eps), None, OP.add)
        k.sig(v, "lnv")
        k.w(s, "lnv")
        k.inc(s.activation(r_row[:, 0:bw], r_row[:, 0:bw], AF.Sqrt), "lns")
        k.w(v, "lns")
        v.reciprocal(r_row[:, 0:bw], r_row[:, 0:bw])
        k.sig(v, "lnv")
        k.w(v, "lnv")
        v.tensor_tensor(m_row[:, 0:bw], m_row[:, 0:bw], r_row[:, 0:bw], OP.mult)
        k.sig(v, "lnv")
        k.w(t, "lnv")
        t.matmul(ps_b1[:, 0:bw], ones1[:], r_row[0:1, 0:bw], start=True, stop=True)
        k.inc(t.matmul(ps_b2[:, 0:bw], ones1[:], m_row[0:1, 0:bw], start=True, stop=True), "lnt")
        k.w(v, "lnt")
        for ci in range(8):
            if ci > 0:
                k.w(v, "lnv")
            v.tensor_tensor(t1[:, 0:bw], xs(ci, sl), ps_b1[:, 0:bw], OP.mult)
            k.sig(v, "lnv")
            k.w(v, "lnv")
            v.tensor_tensor(xns(ci, sl), t1[:, 0:bw], ps_b2[:, 0:bw], OP.subtract)
            k.sig(v, "lnv")
        k.w(t, "lnv")


def ln_scratch(k, TLc=TL, ps4=None, small_sq=False):
    if ps4 is None:
        ps4 = (k.ps("ln_ps_s", [1, 512]), k.ps("ln_ps_q", [1, 512]),
               k.ps("ln_ps_b1", [128, 512]), k.ps("ln_ps_b2", [128, 512]))
    scr = dict(
        sq=k.sb("ln_sq", [128, 1024]),
        ones128=k.sb("ln_o128", [128, 1]),
        ones1=k.sb("ln_o1", [1, 128]),
        ps_s=ps4[0], ps_q=ps4[1], ps_b1=ps4[2], ps_b2=ps4[3],
        m_row=k.sb("ln_m", [1, 512]), r_row=k.sb("ln_r", [1, 512]),
        t1=k.sb("ln_t1", [128, 512]),
    )
    nc = k.nc
    nc.gpsimd.memset(scr["ones128"][:], 1.0)
    nc.gpsimd.memset(scr["ones1"][:], 1.0)
    k.sem("cst")
    k.sig(nc.gpsimd, "cst")
    k.w(nc.tensor, "cst")
    return scr


def build_attn_A(cfg):
    k = K()
    nc = k.nc
    t, v, s, g, sy = nc.tensor, nc.vector, nc.scalar, nc.gpsimd, nc.sync
    NQ = cfg["NQ"]
    NV = cfg["NV"]
    fold = cfg.get("fold")
    xT = k.din("xT", [D, TL])
    if fold == "l0aa":
        wq = k.din("wq", [D, D])
        wk = k.din("wk", [D, D])
        wv = k.din("wv", [D, D])
        Vm = k.din("Vm", [64, 32])
    else:
        wq = k.din("wq", [D, NQ])
        wk = k.din("wk", [D, NQ])
        wv = k.din("wv", [D, NV])
    o_q = k.dout("q", [NQ, TL], BF16)
    o_k = k.dout("kk", [NQ, TL], BF16)
    o_vt = k.dout("vt", [TL, NV], BF16)
    if cfg["bias"]:
        if fold == "l0aa":
            bq = k.din("bq", [128, 8])
            bk = k.din("bk", [128, 8])
            bv = k.din("bv", [128, 8])
        else:
            bq = k.din("bq", [1, NQ])
            bk = k.din("bk", [1, NQ])
            bv = k.din("bv", [1, NV])

    with k.st:
        x_sb = [k.sb(f"x{i}", [128, TL]) for i in range(8)]
        xn_sb = [k.sb(f"xn{i}", [128, TL], BF16) for i in range(8)]
        w_sb = [k.sb(f"w{i}", [128, max(NQ, NV) if fold is None else D], BF16) for i in range(8)]
        if fold == "l0aa":
            wz_sb = [k.sb(f"wz{i}", [128, 512], BF16) for i in range(8)]
            vm_sb = k.sb("vm", [128, 32], BF16)
            br_sb = k.sb("br_sb", [128, 8], BF16)
            ps_f = k.ps("ps_f", [128, 512])
        scr = ln_scratch(k)
        onesrow = k.sb("onesrow", [1, TL], BF16)
        brow = k.sb("brow", [1, max(NQ, NV)], BF16) if cfg["bias"] else None
        pp = k.ps("pp", [128, 512])
        ev1 = k.sb("ev1", [128, 512], BF16)
        k.sem("ld", "wt", "st", "ev", "pjt")
        with nc.Block():
            for i in range(8):
                k.dma(sy, x_sb[i][:], xT[i * 128:(i + 1) * 128, :], "ld")
            g.memset(onesrow[:], 1.0)
            k.sig(g, "cst")
            k.w(v, "ld")
            k.w(t, "ld")
            k.w(s, "ld")
            k.w(t, "cst")
            emit_ln(k, x_sb, xn_sb, scr)

            jobs = [("q", wq, bq if cfg["bias"] else None, o_q, NQ, cfg.get("qscale", 1.0), False),
                    ("k", wk, bk if cfg["bias"] else None, o_k, NQ, 1.0, False),
                    ("v", wv, bv if cfg["bias"] else None, o_vt, NV, 1.0, True)]
            if fold == "l0aa":
                k.dma(g, vm_sb[0:64, :], Vm, "wt")
                k.dma(g, vm_sb[64:128, :], Vm, "wt")
            for (tag, wap, bap, oap, NN, scl, tokmaj) in jobs:
                k.w(g, "wt")
                k.w(g, "pjt")
                for i in range(8):
                    k.dma(g, w_sb[i][:, 0:(NN if fold is None else D)],
                          wap[i * 128:(i + 1) * 128, :], "wt")
                if bap is not None and fold is None:
                    k.dma(g, brow[:, 0:NN], bap, "wt")
                if fold == "l0aa":
                    if bap is not None:
                        k.dma(g, br_sb[:], bap, "wt")
                    k.w(t, "wt")
                    for ch in range(8):
                        for h in range(16):
                            if ch == 0 and h <= 1:
                                k.w(t, "ev")
                            t.matmul(ps_f[:, 0:32],
                                     w_sb[h // 2][(h % 2) * 64:(h % 2) * 64 + 64,
                                                  ch * 128:(ch + 1) * 128],
                                     vm_sb[(h % 2) * 64:(h % 2) * 64 + 64, :],
                                     start=True, stop=True)
                            k.sig(t, "pjt")
                            k.w(v, "pjt")
                            v.tensor_copy(wz_sb[ch][:, h * 32:(h + 1) * 32], ps_f[:, 0:32])
                            k.sig(v, "ev")
                            k.w(t, "ev")
                    if bap is not None:
                        for h in range(16):
                            t.matmul(ps_f[0:1, 0:32],
                                     br_sb[(h % 2) * 64:(h % 2) * 64 + 64, h // 2:h // 2 + 1],
                                     vm_sb[(h % 2) * 64:(h % 2) * 64 + 64, :],
                                     start=True, stop=True)
                            k.sig(t, "pjt")
                            k.w(v, "pjt")
                            v.tensor_copy(brow[0:1, h * 32:(h + 1) * 32], ps_f[0:1, 0:32])
                            k.sig(v, "ev")
                            k.w(t, "ev")
                Wt = wz_sb if fold == "l0aa" else w_sb
                bap_eff = brow if (bap is not None) else None
                k.w(t, "wt")
                k.w(t, "lnv")
                if tokmaj:
                    for tt in range(TL // 128):
                        for cb in range((NN + 511) // 512):
                            cs = slice(cb * 512, min(NN, (cb + 1) * 512))
                            w_ = cs.stop - cs.start
                            for kc in range(8):
                                t.matmul(pp[:, 0:w_], xn_sb[kc][:, tt * 128:(tt + 1) * 128],
                                         Wt[kc][:, cs], start=(kc == 0),
                                         stop=(kc == 7 and bap is None))
                            if bap is not None:
                                t.matmul(pp[:, 0:w_], onesrow[0:1, tt * 128:tt * 128 + 128],
                                         brow[0:1, cs], start=False, stop=True)
                            k.sig(t, "pjt")
                            k.w(v, "pjt")
                            k.w(v, "st")
                            v.tensor_copy(ev1[:, 0:w_], pp[:, 0:w_])
                            k.sig(v, "ev")
                            k.w(sy, "ev")
                            k.dma(sy, oap[tt * 128:(tt + 1) * 128, cs], ev1[:, 0:w_], "st")
                            k.w(t, "ev")
                else:
                    for mt in range((NN + 127) // 128):
                        ms = slice(mt * 128, min(NN, (mt + 1) * 128))
                        mw = ms.stop - ms.start
                        for j in range(NB):
                            ns = slice(j * 512, (j + 1) * 512)
                            for kc in range(8):
                                t.matmul(pp[0:mw, :], Wt[kc][:, ms], xn_sb[kc][:, ns],
                                         start=(kc == 0), stop=(kc == 7 and bap is None))
                            if bap is not None:
                                t.matmul(pp[0:mw, :], brow[0:1, ms], onesrow[0:1, ns],
                                         start=False, stop=True)
                            k.sig(t, "pjt")
                            k.w(v, "pjt")
                            k.w(v, "st")
                            if scl != 1.0:
                                v.tensor_scalar(ev1[0:mw, :], pp[0:mw, :], float(scl), None, OP.mult)
                            else:
                                v.tensor_copy(ev1[0:mw, :], pp[0:mw, :])
                            k.sig(v, "ev")
                            k.w(sy, "ev")
                            k.dma(sy, oap[ms, ns], ev1[0:mw, :], "st")
                            k.w(t, "ev")
            k.w(sy, "st")
            k.w(g, "st")
    k.close()
    return k.nc


def build_attn_B(cfg):
    k = K()
    nc = k.nc
    t, v, s, g, sy = nc.tensor, nc.vector, nc.scalar, nc.gpsimd, nc.sync
    dk, nh = cfg["dk"], 16
    NQ = dk * nh
    NV = cfg["NV"]
    dv = NV // nh
    causal = cfg["causal"]
    dke = dk + 1
    NKT = T // 128
    fold = cfg.get("fold")
    qe = k.din("q", [NQ, TL], BF16)
    kf = k.din("kf", [NQ, T], BF16)
    vtf = k.din("vtf", [nh, 128, NKT * (dv + 1)], BF16)
    wot = k.din("wot", [D if fold == "l0aa" else NV, D])
    Um = k.din("Um", [64, 32]) if fold == "l0aa" else None
    bo = k.din("bo", [1, D]) if cfg["bias"] else None
    xT = k.din("xT", [D, TL])
    y = k.dout("y", [D, TL])

    with k.st:
        q_sb = [k.sb(f"qe{h}", [dke, TL], BF16) for h in range(nh)]
        k_sb = [k.sb(f"ke{i}", [dke, T], BF16) for i in range(2)]
        vt_sb = [k.sb(f"vt{i}", [128, NKT * (dv + 1)], BF16) for i in range(2)]
        oall = [k.sb(f"oall{i}", [128, TL], BF16) for i in range(NV // 128)]
        wo_sb = [k.sb(f"wo{i}", [128, D], BF16) for i in range(NV // 128)]
        if fold == "l0aa":
            wor_sb = [k.sb(f"wor{i}", [128, D], BF16) for i in range(8)]
            um_sb = k.sb("um", [128, 32], BF16)
        borow = k.sb("borow", [1, D], BF16) if cfg["bias"] else None
        onesrow = k.sb("onesrow", [1, TL], BF16)
        sqh = k.sb("sqh", [64, TL])
        normh = k.sb("normh", [1, TL])
        kmaxc = k.sb("kmaxc", [64, 1])
        kmaxb = k.sb("kmaxb", [64, 1], BF16)
        kmx = k.sb("kmx", [1, 1])
        pt_sb = [k.sb(f"pt{i}", [128, 512], BF16) for i in range(2)]
        o_sb = k.sb("o_sb", [128, 64], BF16)
        rq = k.sb("rq", [128, 1])
        xr = k.sb("xr", [128, 512])
        yr = k.sb("yr", [128, 512])
        ones64 = k.sb("ones64", [64, 1])
        ps_s = [k.ps(f"ps_s{i}", [128, 512]) for i in range(2)]
        ps_o = k.ps("ps_o", [128, 2048])
        ps_t = k.ps("ps_t", [64, 128], BF16)
        ps_y = k.ps("ps_y", [128, 512])

        k.sem("ld", "wt", "st", "fT", "fE", "fV", "cst", "pjt", "ev", "bar")
        with nc.Block():
            ident = emit_ident(k)
            msk = emit_diag_masks(k) if causal else None
            g.memset(onesrow[:], 1.0)
            g.memset(ones64[:], 1.0)
            for i in range(2):
                g.memset(k_sb[i][dk:dke, :], 1.0)
            k.sig(g, "cst")
            for h in range(nh):
                k.dma(sy, q_sb[h][0:dk, :], qe[h * dk:(h + 1) * dk, :], "ld")
            if fold == "l0aa":
                for i in range(8):
                    k.dma(g, wor_sb[i][:], wot[i * 128:(i + 1) * 128, :], "wt")
                k.dma(g, um_sb[0:64, :], Um, "wt")
                k.dma(g, um_sb[64:128, :], Um, "wt")
            else:
                for i in range(NV // 128):
                    k.dma(g, wo_sb[i][:], wot[i * 128:(i + 1) * 128, :], "wt")
            if cfg["bias"]:
                k.dma(g, borow[:], bo, "wt")
            k.w(t, "cst")
            k.w(v, "cst")
            k.w(s, "cst")

            def flash_for_core(core):
                b, (c0, c1) = core_chunks(core)
                qpos_of = [c0 * 1024, c0 * 1024 + 512, c1 * 1024, c1 * 1024 + 512]
                for h in range(nh):
                    hb = h % 2
                    k.w(sy, "fV", k.c["fV"])
                    k.dma(sy, k_sb[hb][0:dk, :], kf[h * dk:(h + 1) * dk, :], "ld")
                    k.dma(sy, vt_sb[hb][:], vtf[h], "ld")
                    k.w(v, "ld")
                    k.w(v, "fT")
                    v.tensor_reduce(kmaxc[0:dk, :], k_sb[hb][0:dk, :], AX.X, OP.max,
                                    apply_absolute_value=True)
                    k.sig(v, "fE")
                    k.w(v, "fE")
                    v.tensor_copy(kmaxb[0:dk, :], kmaxc[0:dk, :])
                    k.sig(v, "fE")
                    k.w(t, "fE")
                    t.transpose(ps_t[0:1, 0:dk], kmaxb[0:dk, 0:1], ident[0:dk, 0:dk])
                    k.sig(t, "fT")
                    k.w(v, "fT")
                    v.tensor_reduce(kmx[:], ps_t[0:1, 0:dk], AX.X, OP.max)
                    k.sig(v, "fE")
                    v.tensor_tensor(sqh[0:dk, :], q_sb[h][0:dk, :], q_sb[h][0:dk, :], OP.mult)
                    k.sig(v, "fE")
                    k.w(t, "fE")
                    for j in range(NB):
                        if j > 0:
                            k.w(t, "fE")
                        t.matmul(ps_y[0:1, :], ones64[0:dk, :], sqh[0:dk, j * 512:(j + 1) * 512],
                                 start=True, stop=True)
                        k.sig(t, "fT")
                        k.w(s, "fT")
                        k.inc(s.activation(normh[0:1, j * 512:(j + 1) * 512], ps_y[0:1, :],
                                           AF.Sqrt), "fE")
                    k.w(v, "fE")
                    k.w(v, "fE")
                    v.tensor_scalar(q_sb[h][dk:dke, :], normh[:], kmx[0:1, 0:1],
                                    -float(np.sqrt(dk)), OP.mult, OP.mult)
                    k.sig(v, "fV")
                    k.w(t, "fV")

                    for qs in range(4):
                        qpos = qpos_of[qs]
                        nk = (qpos + 512) // 128 if causal else NKT
                        exp_done = []
                        av_done = []
                        for j in range(nk):
                            pb = j % 2
                            if j >= 2:
                                k.w(t, "fE", exp_done[j - 2])
                                if causal and (j - 2) >= nk - 4:
                                    k.w(t, "fV", mask_done[j - 2 - (nk - 4)])
                            mm = t.matmul(ps_s[pb][:], k_sb[hb][:, j * 128:(j + 1) * 128],
                                          q_sb[h][:, qs * 512:(qs + 1) * 512],
                                          start=True, stop=True)
                            k.inc(mm, "fT")
                            sT = k.c["fT"]
                            if j >= 2:
                                k.w(s, "fT", av_done[j - 2])
                            k.w(s, "fT", sT)
                            k.inc(s.activation(pt_sb[pb][:], ps_s[pb][:], AF.Exp), "fE")
                            exp_done.append(k.c["fE"])
                            if j == 0:
                                mask_done = []
                            masked = causal and j >= nk - 4
                            if masked:
                                k.w(v, "fE", exp_done[-1])
                                mi = j - (nk - 4)
                                v.tensor_tensor(pt_sb[pb][:], pt_sb[pb][:],
                                                msk[:, mi * 512:(mi + 1) * 512], OP.mult)
                                k.sig(v, "fV")
                                mask_done.append(k.c["fV"])
                                k.w(t, "fV")
                            else:
                                k.w(t, "fE", exp_done[-1])
                            for jq in range(4):
                                t.matmul(ps_o[:, jq * 512:jq * 512 + dv + 1],
                                         pt_sb[pb][:, jq * 128:(jq + 1) * 128],
                                         vt_sb[hb][:, j * (dv + 1):(j + 1) * (dv + 1)],
                                         start=(j == 0), stop=(j == nk - 1))
                            k.sig(t, "fT")
                            av_done.append(k.c["fT"])
                        k.w(v, "fT")
                        for jq in range(4):
                            v.reciprocal(rq[:], ps_o[:, jq * 512 + dv:jq * 512 + dv + 1])
                            k.sig(v, "fV")
                            k.w(v, "fV")
                            v.tensor_scalar(o_sb[:, 0:dv], ps_o[:, jq * 512:jq * 512 + dv],
                                            rq[:], None, OP.mult)
                            k.sig(v, "fV")
                            k.w(t, "fV")
                            t.transpose(ps_t[0:dv, :], o_sb[:, 0:dv], ident[:])
                            k.sig(t, "fT")
                            k.w(v, "fT")
                            ht = (h * dv) // 128
                            hr = (h * dv) % 128
                            v.tensor_copy(
                                oall[ht][hr:hr + dv,
                                         qs * 512 + jq * 128:qs * 512 + (jq + 1) * 128],
                                ps_t[0:dv, :])
                            k.sig(v, "fV")
                        k.w(t, "fV")

            if causal:
                engs = [t, v, s, sy, g]
                pids = [e.partition_id() for e in engs]
                snap = dict(k.c)
                pre_bar = k.c["bar"]
                final = None
                for core in nc.Switch(engines=engs, index=pids, n=8):
                    for n_ in snap:
                        k.c[n_] = dict(snap)[n_]
                    flash_for_core(core)
                    for e_ in (t, s, v, sy, g):
                        k.w(e_, "fT")
                        k.w(e_, "fE")
                        k.w(e_, "fV")
                    t.maybe_drain_then_inc((k.S["bar"], 1))
                    s.maybe_drain_then_inc((k.S["bar"], 1))
                    v.maybe_drain_then_inc((k.S["bar"], 1))
                    g.maybe_drain_then_inc((k.S["bar"], 1))
                    sy.wait_ge(k.S["bar"], pre_bar + 4)
                    sy.sem_clear(k.S["fT"])
                    sy.sem_clear(k.S["fE"])
                    sy.sem_clear(k.S["fV"])
                    sy.maybe_drain_then_inc((k.S["bar"], 1))
                    final = dict(k.c)
                for n_ in snap:
                    k.c[n_] = snap[n_]
                k.c["bar"] = pre_bar + 5
                k.c["fT"] = k.c["fE"] = k.c["fV"] = 0
                k.c["ld"] = final["ld"]
                for e_ in (t, v, s, sy, g):
                    k.w(e_, "bar")
            else:
                flash_for_core(0)

            if fold == "l0aa":
                k.sem("wf")
                k.w(t, "wt")
                for h in range(nh):
                    for n2 in range(2):
                        t.matmul(ps_y[0:32, :],
                                 um_sb[(h % 2) * 64:(h % 2) * 64 + 64, :],
                                 wor_sb[h // 2][(h % 2) * 64:(h % 2) * 64 + 64,
                                                n2 * 512:(n2 + 1) * 512],
                                 start=True, stop=True)
                        k.sig(t, "wf")
                        k.w(v, "wf")
                        v.tensor_copy(wo_sb[h // 4][(h % 4) * 32:(h % 4) * 32 + 32,
                                                    n2 * 512:(n2 + 1) * 512],
                                      ps_y[0:32, :])
                        k.sig(v, "wf")
                        k.w(t, "wf")
            k.w(t, "fV")
            k.w(t, "wt")
            for ft in range(8):
                for j in range(NB):
                    ns = slice(j * 512, (j + 1) * 512)
                    k.w(v, "st")
                    k.w(sy, "ev")
                    k.dma(sy, xr[:], xT[ft * 128:(ft + 1) * 128, ns], "ld")
                    for hp in range(NV // 128):
                        t.matmul(ps_y[:], wo_sb[hp][:, ft * 128:(ft + 1) * 128],
                                 oall[hp][:, ns], start=(hp == 0),
                                 stop=(hp == NV // 128 - 1 and not cfg["bias"]))
                    if cfg["bias"]:
                        t.matmul(ps_y[:], borow[0:1, ft * 128:(ft + 1) * 128],
                                 onesrow[0:1, ns], start=False, stop=True)
                    k.sig(t, "pjt")
                    k.w(v, "pjt")
                    k.w(v, "ld")
                    v.tensor_tensor(yr[:], xr[:], ps_y[:], OP.add)
                    k.sig(v, "ev")
                    k.w(t, "ev")
                    k.w(sy, "ev")
                    k.dma(sy, y[ft * 128:(ft + 1) * 128, ns], yr[:], "st")
            k.w(sy, "st")
            k.w(g, "st")
    k.close()
    return k.nc


def x_to_cores(x):
    out = []
    for c in range(NC):
        b, (c0, c1) = core_chunks(c)
        cols = np.concatenate([x[b, c0 * 1024:(c0 + 1) * 1024, :],
                               x[b, c1 * 1024:(c1 + 1) * 1024, :]], axis=0)
        out.append(np.ascontiguousarray(cols.T))
    return out


def cores_to_x(ys):
    x = np.empty((2, T, D), np.float32)
    for c in range(NC):
        b, (c0, c1) = core_chunks(c)
        yt = ys[c].T
        x[b, c0 * 1024:(c0 + 1) * 1024, :] = yt[:1024]
        x[b, c1 * 1024:(c1 + 1) * 1024, :] = yt[1024:]
    return x


def gather_heads_k(ks, b):
    NQ = ks[0].shape[0]
    kf = np.empty((NQ, T), dtype=ks[0].dtype)
    for q in range(4):
        c = b * 4 + q
        kf[:, q * 1024:(q + 1) * 1024] = ks[c][:, 0:1024]
        kf[:, (7 - q) * 1024:(8 - q) * 1024] = ks[c][:, 1024:2048]
    return kf


def gather_vt(vts, b, nh, dv):
    NKT = T // 128
    vfull = np.empty((T, nh * dv), dtype=vts[0].dtype)
    for q in range(4):
        c = b * 4 + q
        vfull[q * 1024:(q + 1) * 1024] = vts[c][0:1024]
        vfull[(7 - q) * 1024:(8 - q) * 1024] = vts[c][1024:2048]
    out = np.ones((nh, 128, NKT, dv + 1), dtype=vts[0].dtype)
    v4 = vfull.reshape(NKT, 128, nh, dv)
    out[:, :, :, 0:dv] = v4.transpose(2, 1, 0, 3)
    return out.reshape(nh, 128, NKT * (dv + 1))


def run_attn_sublayer(x, cfgA, cfgB, wq, wk, wv, bq, bk, bv, wot, bo,
                      extraA=None, extraB=None):
    xts = x_to_cores(x)
    ncA = build_attn_A(cfgA)
    insA = []
    for c in range(NC):
        d = {"xT": xts[c], "wq": wq, "wk": wk, "wv": wv}
        if cfgA["bias"]:
            if cfgA.get("fold") == "l0aa":
                d.update({"bq": bq.reshape(8, 128).T.copy(),
                          "bk": bk.reshape(8, 128).T.copy(),
                          "bv": bv.reshape(8, 128).T.copy()})
            else:
                d.update({"bq": bq[None, :], "bk": bk[None, :], "bv": bv[None, :]})
        if extraA:
            d.update(extraA)
        insA.append(d)
    resA = run_bass_kernel_spmd(ncA, insA, core_ids=list(range(NC)))
    qs = [resA.results[c]["q"] for c in range(NC)]
    ks = [resA.results[c]["kk"] for c in range(NC)]
    vts = [resA.results[c]["vt"] for c in range(NC)]
    nh, dv = 16, cfgB["NV"] // 16
    kfs = [gather_heads_k(ks, b) for b in range(2)]
    vtfs = [gather_vt(vts, b, nh, dv) for b in range(2)]
    ncB = build_attn_B(cfgB)
    insB = []
    for c in range(NC):
        b = c // 4
        d = {"q": qs[c], "kf": kfs[b], "vtf": vtfs[b], "wot": wot, "xT": xts[c]}
        if cfgB["bias"]:
            d["bo"] = bo[None, :]
        if extraB:
            d.update(extraB)
        insB.append(d)
    resB = run_bass_kernel_spmd(ncB, insB, core_ids=list(range(NC)))
    return cores_to_x([resB.results[c]["y"] for c in range(NC)])


def build_ffn_attn(cfg):
    k = K()
    nc = k.nc
    t, v, s, g, sy = nc.tensor, nc.vector, nc.scalar, nc.gpsimd, nc.sync
    HE, CW = 8, 1537
    zT = k.din("zT", [D, 2 * CW])
    wqkv = k.din("wqkv", [D, 1536])
    wo = k.din("wo", [512, D])
    qs_i = k.din("qs", [128, 1])
    qsp_i = k.din("qsp", [128, 1])
    ks_i = k.din("ks", [128, 1])
    ksp_i = k.din("ksp", [128, 1])
    qt1_i = k.din("qt1", [128, 2 * 1024])
    qt2_i = k.din("qt2", [128, 2 * 1024])
    kt1_i = k.din("kt1", [128, 2 * 1536])
    kt2_i = k.din("kt2", [128, 2 * 1536])
    s01_i = k.din("s01i", [128, 4])
    yT = k.dout("y", [D, TL])
    LNB = [(0, 512), (512, 512), (1024, 512), (1536, 1)]

    with k.st:
        z_sb = [k.sb(f"z{i}", [128, CW]) for i in range(8)]
        xn_sb = [k.sb(f"xn{i}", [128, CW], BF16) for i in range(8)]
        w_sb = [k.sb(f"w{i}", [128, 1536], BF16) for i in range(8)]
        wo_sb = [k.sb(f"wo{i}", [128, D], BF16) for i in range(4)]
        qf = k.sb("qf", [128, 4 * 1024], BF16)
        kf_ = k.sb("kf", [128, 4 * 1536], BF16)
        vt_sb = k.sb("vt", [128, 12 * HE * 65], BF16)
        sqv = k.sb("sqv", [128, 1536], BF16)
        nrm = k.sb("nrm", [1, 2 * 1536])
        ones1b = k.sb("ones1b", [1, 64], BF16)
        ones128b = k.sb("ones128b", [128, 1], BF16)
        tq1 = k.sb("tq1", [128, 1024], BF16)
        tq2 = k.sb("tq2", [128, 1024], BF16)
        tk1 = k.sb("tk1", [128, 1536], BF16)
        tk2 = k.sb("tk2", [128, 1536], BF16)
        rot = k.sb("rot", [128, 1536], BF16)
        tmp1 = k.sb("tmp1", [128, 1536], BF16)
        qs_sb = k.sb("qs_sb", [128, 4])
        s01 = k.sb("s01", [128, 4])
        pt_sb = k.sb("ptb", [128, 1024], BF16)
        ptt = k.sb("ptt", [128, 128], BF16)
        o_sb = k.sb("o_sb", [128, 64], BF16)
        oat = [k.sb(f"oat{i}", [128, 1024], BF16) for i in range(4)]
        rq = k.sb("rq", [128, 1])
        rmaxn = k.sb("rmaxn", [128, 2])
        yr = k.sb("yr", [128, 512])
        ps_s0 = k.ps("ps_s0", [128, 512])
        ps_s1 = k.ps("ps_s1", [128, 512])
        ps_o = k.ps("ps_o", [128, 512])
        ps_t = k.ps("ps_t", [128, 128], BF16)
        ps_y = k.ps("ps_y", [128, 512])
        scr = ln_scratch(k, TLc=CW, ps4=(ps_s0[0:1, :], ps_s1[0:1, :], ps_o, ps_y), small_sq=True)
        k.sem("ld", "ld2", "wt", "st", "fT", "fE", "fV", "cst", "pjt", "ev", "lnv", "lnt")
        msk1 = k.sb("c_msk1", [128, 4 * 512], BF16)
        msk = k.sb("c_msk0", [128, 4 * 512], BF16)
        with nc.Block():
            ident = emit_ident(k)
            mv2 = k.sb("c_mv2", [128, 512], I16)
            g.iota(mv2[:], [[1, 512]], base=0, channel_multiplier=-1,
                   allow_small_or_imprecise_dtypes=True)
            k.sig(g, "cst")
            k.w(v, "cst")
            for jm, dd in enumerate((0, 128, 256, 384)):
                v.tensor_scalar(msk[:, jm * 512:(jm + 1) * 512], mv2[:], float(dd),
                                None, OP.is_ge)
                v.tensor_scalar(msk1[:, jm * 512:(jm + 1) * 512], mv2[:], float(dd),
                                None, OP.is_le)
            k.sig(v, "cst")
            g.memset(vt_sb[:], 1.0)
            g.memset(nrm[:], 1.0)
            g.memset(ones1b[:], 1.0)
            g.memset(ones128b[:], 1.0)
            k.sig(g, "cst")
            for i in range(8):
                k.dma(g, w_sb[i][:], wqkv[i * 128:(i + 1) * 128, :], "wt")
            for i in range(4):
                k.dma(g, wo_sb[i][:], wo[i * 128:(i + 1) * 128, :], "wt")
            k.dma(g, s01[:], s01_i, "wt")
            k.dma(g, qs_sb[:, 0:1], qs_i, "wt")
            k.dma(g, qs_sb[:, 1:2], qsp_i, "wt")
            k.dma(g, qs_sb[:, 2:3], ks_i, "wt")
            k.dma(g, qs_sb[:, 3:4], ksp_i, "wt")
            for e_ in (t, v, s):
                k.w(e_, "cst")
                k.w(e_, "wt")

            for ch in range(2):
                k.w(sy, "fT", k.c["fT"])
                k.w(sy, "lnv", k.c["lnv"])
                k.w(sy, "ev", k.c["ev"])
                for i in range(8):
                    k.dma(sy, z_sb[i][:], zT[i * 128:(i + 1) * 128, ch * CW:(ch + 1) * CW], "ld")
                k.w(g, "fV", k.c["fV"])
                k.dma(g, tq1[:], qt1_i[:, ch * 1024:(ch + 1) * 1024], "ld2")
                k.dma(g, tq2[:], qt2_i[:, ch * 1024:(ch + 1) * 1024], "ld2")
                k.dma(g, tk1[:], kt1_i[:, ch * 1536:(ch + 1) * 1536], "ld2")
                k.dma(g, tk2[:], kt2_i[:, ch * 1536:(ch + 1) * 1536], "ld2")
                for e_ in (t, v, s):
                    k.w(e_, "ld")
                    k.w(e_, "ld2")
                v.tensor_scalar(tq1[:], tq1[:], qs_sb[:, 0:1], None, OP.mult)
                k.sig(v, "fV")
                v.tensor_scalar(tq2[:], tq2[:], qs_sb[:, 1:2], None, OP.mult)
                k.sig(v, "fV")
                v.tensor_scalar(tk1[:], tk1[:], qs_sb[:, 2:3], None, OP.mult)
                k.sig(v, "fV")
                v.tensor_scalar(tk2[:], tk2[:], qs_sb[:, 3:4], None, OP.mult)
                k.sig(v, "fV")
                k.w(v, "fV")

                k.w(v, "pjt")
                k.w(t, "fE")
                if cfg["outer_ln"]:
                    emit_ln(k, z_sb, xn_sb, scr, TLc=CW, blocks=LNB)
                else:
                    for i in range(8):
                        v.tensor_copy(xn_sb[i][:], z_sb[i][:])
                    k.sig(v, "lnv")
                k.w(t, "lnv")

                for (tag, c0, ncols, wofs, dst) in (("q", 513, 1024, 0, qf),
                                                    ("k", 1, 1536, 512, kf_)):
                    for mt in range(4):
                        for nb2 in range(ncols // 512):
                            for kc in range(8):
                                off = c0 + nb2 * 512 - (1 if kc >= 4 else 0)
                                t.matmul(ps_y[:], w_sb[kc][:, wofs + mt * 128:wofs + (mt + 1) * 128],
                                         xn_sb[kc][:, off:off + 512],
                                         start=(kc == 0), stop=(kc == 7))
                            k.sig(t, "pjt")
                            k.w(v, "pjt")
                            v.tensor_copy(dst[:, mt * ncols + nb2 * 512:mt * ncols + nb2 * 512 + 512],
                                          ps_y[:])
                            k.sig(v, "ev")
                            k.w(t, "ev")
                for tt in range(12):
                    for kc in range(8):
                        off = 1 + tt * 128 - (1 if kc >= 4 else 0)
                        t.matmul(ps_y[:], xn_sb[kc][:, off:off + 128],
                                 w_sb[kc][:, 1024:1536], start=(kc == 0), stop=(kc == 7))
                    k.sig(t, "pjt")
                    k.w(v, "pjt")
                    for h in range(HE):
                        v.tensor_copy(vt_sb[:, (tt * HE + h) * 65:(tt * HE + h) * 65 + 64],
                                      ps_y[:, h * 64:(h + 1) * 64])
                    k.sig(v, "ev")
                    k.w(t, "ev")

                for (dst, ncols, t1_, t2_) in ((qf, 1024, tq1, tq2), (kf_, 1536, tk1, tk2)):
                    for mt in range(4):
                        base = mt * ncols
                        sl = slice(base, base + ncols)
                        k.w(v, "fT")
                        v.tensor_tensor(sqv[:, 0:ncols], dst[:, sl], dst[:, sl], OP.mult)
                        k.sig(v, "fV")
                        k.w(t, "fV")
                        for h2 in range(2):
                            for nb2 in range(ncols // 512):
                                if not (h2 == 0 and nb2 == 0):
                                    k.w(t, "fE")
                                t.matmul(ps_y[0:1, 0:512],
                                         ones128b[h2 * 64:h2 * 64 + 64, :],
                                         sqv[h2 * 64:h2 * 64 + 64, nb2 * 512:nb2 * 512 + 512],
                                         start=True, stop=True)
                                k.sig(t, "fT")
                                k.w(s, "fT")
                                k.inc(s.activation(nrm[0:1, h2 * 1536 + nb2 * 512:h2 * 1536 + nb2 * 512 + 512],
                                                   ps_y[0:1, 0:512], AF.Sqrt), "fE")
                        k.w(v, "fE")
                        v.tensor_scalar(nrm[:], nrm[:], 1e-12, None, OP.max)
                        k.sig(v, "fV")
                        k.w(v, "fV")
                        v.reciprocal(nrm[:], nrm[:])
                        k.sig(v, "fV")
                        k.w(t, "fV")
                        for h2 in range(2):
                            for nb2 in range(ncols // 512):
                                k.w(t, "fV")
                                t.matmul(ps_s0[0:64, :] if h2 == 0 else ps_s0[64:128, :],
                                         scr["ones1"][0:1, 0:64],
                                         nrm[0:1, h2 * 1536 + nb2 * 512:h2 * 1536 + nb2 * 512 + 512],
                                         start=True, stop=True,
                                         tile_position=(0, 0) if h2 == 0 else (0, 64))
                                k.sig(t, "fT")
                                k.w(v, "fT")
                                v.tensor_tensor(tmp1[h2 * 64:h2 * 64 + 64, nb2 * 512:nb2 * 512 + 512],
                                                dst[h2 * 64:h2 * 64 + 64,
                                                    base + nb2 * 512:base + nb2 * 512 + 512],
                                                ps_s0[0:64, :] if h2 == 0 else ps_s0[64:128, :],
                                                OP.mult)
                                k.sig(v, "fV")
                        k.w(v, "fV")
                        for hb in (0, 64):
                            v.tensor_scalar(rot[hb:hb + 32, 0:ncols], tmp1[hb + 32:hb + 64, 0:ncols],
                                            -1.0, None, OP.mult)
                            k.sig(v, "fV")
                            v.tensor_copy(rot[hb + 32:hb + 64, 0:ncols], tmp1[hb:hb + 32, 0:ncols])
                            k.sig(v, "fV")
                        k.w(v, "fV")
                        v.tensor_tensor(tmp1[:, 0:ncols], tmp1[:, 0:ncols], t1_[:, 0:ncols], OP.mult)
                        k.sig(v, "fV")
                        v.tensor_tensor(rot[:, 0:ncols], rot[:, 0:ncols], t2_[:, 0:ncols], OP.mult)
                        k.sig(v, "fV")
                        k.w(v, "fV")
                        v.tensor_tensor(sqv[:, 0:ncols], tmp1[:, 0:ncols], rot[:, 0:ncols], OP.add)
                        k.sig(v, "fV")
                        k.w(v, "fV")
                        v.tensor_copy(dst[:, sl], sqv[:, 0:ncols])
                        k.sig(v, "fV")
                        k.w(t, "fV")

                for qs2 in range(2):
                    for h in range(HE):
                        hp, hh = h // 2, (h % 2) * 64
                        for qt4 in range(4):
                            qcol = hp * 1024 + qs2 * 512 + qt4 * 128
                            kbase = hp * 1536 + qs2 * 512
                            k.w(t, "fV")
                            for bank, pss in ((0, ps_s0), (1, ps_s1)):
                                t.matmul(pss[:], qf[hh:hh + 64, qcol:qcol + 128],
                                         kf_[hh:hh + 64, kbase + bank * 512:kbase + bank * 512 + 512],
                                         start=True, stop=True)
                            k.sig(t, "fT")
                            k.w(v, "fT")
                            v.tensor_reduce(rmaxn[:, 0:1], ps_s0[:], AX.X, OP.max)
                            k.sig(v, "fV")
                            v.tensor_reduce(rmaxn[:, 1:2], ps_s1[:], AX.X, OP.max)
                            k.sig(v, "fV")
                            k.w(v, "fV")
                            v.tensor_tensor(rmaxn[:, 0:1], rmaxn[:, 0:1], rmaxn[:, 1:2], OP.max)
                            k.sig(v, "fV")
                            k.w(v, "fV")
                            v.tensor_scalar(rmaxn[:, 1:2], rmaxn[:, 0:1], -1.0, None, OP.mult)
                            k.sig(v, "fV")
                            k.w(s, "fV")
                            k.inc(s.activation(pt_sb[:, 0:512], ps_s0[:], AF.Exp,
                                               bias=rmaxn[:, 1:2]), "fE")
                            k.inc(s.activation(pt_sb[:, 512:1024], ps_s1[:], AF.Exp,
                                               bias=rmaxn[:, 1:2]), "fE")
                            k.w(v, "fE")
                            v.tensor_scalar(pt_sb[:, 0:512], pt_sb[:, 0:512],
                                            s01[:, ch * 2 + qs2:ch * 2 + qs2 + 1], None, OP.mult)
                            k.sig(v, "fV")
                            k.w(v, "fV")
                            v.tensor_tensor(pt_sb[:, 0:512], pt_sb[:, 0:512],
                                            msk[:, qt4 * 512:(qt4 + 1) * 512], OP.mult)
                            k.sig(v, "fV")
                            v.tensor_tensor(pt_sb[:, 512:1024], pt_sb[:, 512:1024],
                                            msk1[:, qt4 * 512:(qt4 + 1) * 512], OP.mult)
                            k.sig(v, "fV")
                            k.w(t, "fV")
                            for kt8 in range(8):
                                t.transpose(ps_t[:], pt_sb[:, kt8 * 128:(kt8 + 1) * 128], ident[:])
                                k.sig(t, "fT")
                                k.w(v, "fT")
                                v.tensor_copy(ptt[:], ps_t[:])
                                k.sig(v, "fV")
                                k.w(t, "fV")
                                vtile = (qs2 * 4 + kt8) * HE + h
                                t.matmul(ps_o[:, 0:65], ptt[:],
                                         vt_sb[:, vtile * 65:vtile * 65 + 65],
                                         start=(kt8 == 0), stop=(kt8 == 7))
                                k.sig(t, "fT")
                            k.w(v, "fT")
                            v.reciprocal(rq[:], ps_o[:, 64:65])
                            k.sig(v, "fV")
                            k.w(v, "fV")
                            v.tensor_scalar(o_sb[:], ps_o[:, 0:64], rq[:], None, OP.mult)
                            k.sig(v, "fV")
                            k.w(t, "fV")
                            t.transpose(ps_t[0:64, :], o_sb[:], ident[:])
                            k.sig(t, "fT")
                            k.w(v, "fT")
                            v.tensor_copy(oat[h // 2][hh:hh + 64,
                                          qs2 * 512 + qt4 * 128:qs2 * 512 + (qt4 + 1) * 128],
                                          ps_t[0:64, :])
                            k.sig(v, "fV")
                            k.w(t, "fV")

                k.w(t, "fV")
                k.w(t, "wt")
                for ft in range(8):
                    for j2 in range(2):
                        ns = slice(ch * 1024 + j2 * 512, ch * 1024 + j2 * 512 + 512)
                        zc = 513 + j2 * 512
                        k.w(v, "st")
                        for hp in range(4):
                            t.matmul(ps_y[:], wo_sb[hp][:, ft * 128:(ft + 1) * 128],
                                     oat[hp][:, j2 * 512:(j2 + 1) * 512],
                                     start=(hp == 0), stop=(hp == 3))
                        k.sig(t, "pjt")
                        k.w(v, "pjt")
                        res = xn_sb[ft][:, zc:zc + 512] if cfg["outer_ln"] else z_sb[ft][:, zc:zc + 512]
                        v.tensor_tensor(yr[:], res, ps_y[:], OP.add)
                        k.sig(v, "ev")
                        k.w(t, "ev")
                        k.w(sy, "ev")
                        k.dma(sy, yT[ft * 128:(ft + 1) * 128, ns], yr[:], "st")
            k.w(sy, "st")
            k.w(g, "st")
    k.close()
    return k.nc


def build_ffn_mlp(cfg):
    k = K()
    nc = k.nc
    t, v, s, g, sy = nc.tensor, nc.vector, nc.scalar, nc.gpsimd, nc.sync
    CW = 1025
    HID = 2730
    zT = k.din("zT", [D, 2 * CW])
    lg_i = k.din("lg", [128, 8])
    lb_i = k.din("lb", [128, 8])
    w1 = k.din("w1", [D, 2 * HID])
    b1 = k.din("b1", [1, 2 * HID])
    w2 = k.din("w2", [HID, D])
    b2 = k.din("b2", [1, D])
    outer = cfg["outer"]
    xo = k.din("xo", [D, TL]) if outer else None
    yT = k.dout("y", [D, TL])
    CC = [128] * 21 + [42]
    CST = [sum(CC[:i]) for i in range(22)]

    with k.st:
        z_sb = [k.sb(f"z{i}", [128, CW]) for i in range(8)]
        xn_sb = [k.sb(f"xn{i}", [128, 1024], BF16) for i in range(8)]
        scr = ln_scratch(k, TLc=1024)
        lg_sb = k.sb("lg_sb", [128, 8])
        lb_sb = k.sb("lb_sb", [128, 8])
        w1a = [k.sb(f"w1a{i}", [128, 512], BF16) for i in range(8)]
        w1g = [k.sb(f"w1g{i}", [128, 512], BF16) for i in range(8)]
        w2_sb = [k.sb(f"w2_{i}", [128, D], BF16) for i in range(22)]
        b1_sb = k.sb("b1_sb", [1, 2 * HID], BF16)
        b2_sb = k.sb("b2_sb", [1, D], BF16)
        h2_sb = [k.sb(f"h2_{i}", [128, 1024], BF16) for i in range(22)]
        gel = k.sb("gel", [128, 512], BF16)
        onesrow = k.sb("onesr", [1, 1024], BF16)
        yr = k.sb("yr", [128, 512])
        xr = k.sb("xr", [128, 512])
        ps_a = k.ps("ps_a", [128, 512])
        ps_g = k.ps("ps_g", [128, 512])
        pp = k.ps("pp", [128, 512])
        k.sem("ld", "wt", "st", "pjt", "ev", "gv", "lnv", "cst", "xld")
        with nc.Block():
            g.memset(onesrow[:], 1.0)
            k.sig(g, "cst")
            k.dma(g, lg_sb[:], lg_i, "wt")
            k.dma(g, lb_sb[:], lb_i, "wt")
            k.dma(g, b1_sb[:], b1, "wt")
            k.dma(g, b2_sb[:], b2, "wt")
            for e_ in (t, v, s):
                k.w(e_, "cst")
                k.w(e_, "wt")
            for tch in range(2):
                k.w(sy, "lnv", k.c["lnv"])
                k.w(sy, "ev", k.c["ev"])
                for i in range(8):
                    k.dma(sy, z_sb[i][:], zT[i * 128:(i + 1) * 128, tch * CW:(tch + 1) * CW], "ld")
                for e_ in (t, v, s):
                    k.w(e_, "ld")
                k.w(v, "pjt")
                offs = [1 if i < 4 else 0 for i in range(8)]
                emit_ln(k, z_sb, xn_sb, scr, TLc=1024, xoffs=offs)
                for ci in range(8):
                    k.w(v, "lnv")
                    v.tensor_scalar(xn_sb[ci][:], xn_sb[ci][:], lg_sb[:, ci:ci + 1],
                                    lb_sb[:, ci:ci + 1], OP.mult, OP.add)
                    k.sig(v, "lnv")
                k.w(t, "lnv")

                for mt6 in range(6):
                    c0 = mt6 * 512
                    cw = min(512, HID - c0)
                    k.w(g, "wt")
                    k.w(g, "pjt")
                    for i in range(8):
                        k.dma(g, w1a[i][:, 0:cw], w1[i * 128:(i + 1) * 128, c0:c0 + cw], "wt")
                        k.dma(g, w1g[i][:, 0:cw],
                              w1[i * 128:(i + 1) * 128, HID + c0:HID + c0 + cw], "wt")
                    k.w(t, "wt")
                    for ms in range(0, cw, 128):
                        mw = min(128, cw - ms)
                        mtg = (c0 + ms) // 128
                        for nb2 in range(2):
                            nsl = slice(nb2 * 512, nb2 * 512 + 512)
                            for kc in range(8):
                                t.matmul(ps_a[0:mw, :], w1a[kc][:, ms:ms + mw],
                                         xn_sb[kc][:, nsl], start=(kc == 0), stop=False)
                                t.matmul(ps_g[0:mw, :], w1g[kc][:, ms:ms + mw],
                                         xn_sb[kc][:, nsl], start=(kc == 0), stop=False)
                            t.matmul(ps_a[0:mw, :], b1_sb[0:1, c0 + ms:c0 + ms + mw],
                                     onesrow[0:1, 0:512], start=False, stop=True)
                            t.matmul(ps_g[0:mw, :], b1_sb[0:1, HID + c0 + ms:HID + c0 + ms + mw],
                                     onesrow[0:1, 0:512], start=False, stop=True)
                            k.sig(t, "pjt")
                            k.w(s, "pjt")
                            k.inc(s.activation(gel[0:mw, :], ps_g[0:mw, :], AF.Gelu), "gv")
                            k.w(v, "gv")
                            k.w(v, "pjt")
                            v.tensor_tensor(h2_sb[mtg][0:mw, nb2 * 512:nb2 * 512 + 512],
                                            ps_a[0:mw, :], gel[0:mw, :], OP.mult)
                            k.sig(v, "ev")
                            k.w(t, "ev")
                            k.w(s, "ev")
                k.w(g, "pjt")
                for i in range(22):
                    k.dma(g, w2_sb[i][0:CC[i], :], w2[CST[i]:CST[i] + CC[i], :], "wt")
                k.w(t, "wt")
                k.w(t, "ev")
                for ft in range(8):
                    for nb2 in range(2):
                        ns = slice(tch * 1024 + nb2 * 512, tch * 1024 + nb2 * 512 + 512)
                        zc = 1 + nb2 * 512
                        k.w(v, "st")
                        if outer:
                            k.w(sy, "xld", k.c["xld"])
                            k.w(sy, "ev", k.c["ev"])
                            k.dma(sy, xr[:], xo[ft * 128:(ft + 1) * 128, ns], "xld")
                        for ki in range(22):
                            t.matmul(pp[:], w2_sb[ki][0:CC[ki], ft * 128:(ft + 1) * 128],
                                     h2_sb[ki][0:CC[ki], nb2 * 512:nb2 * 512 + 512],
                                     start=(ki == 0), stop=False)
                        t.matmul(pp[:], b2_sb[0:1, ft * 128:(ft + 1) * 128],
                                 onesrow[0:1, 0:512], start=False, stop=True)
                        k.sig(t, "pjt")
                        k.w(v, "pjt")
                        v.tensor_tensor(yr[:], z_sb[ft][:, zc:zc + 512], pp[:], OP.add)
                        k.sig(v, "ev")
                        if outer:
                            k.w(v, "xld")
                            k.w(v, "ev")
                            v.tensor_tensor(yr[:], yr[:], xr[:], OP.add)
                            k.sig(v, "ev")
                        k.w(t, "ev")
                        k.w(sy, "ev")
                        k.dma(sy, yT[ft * 128:(ft + 1) * 128, ns], yr[:], "st")
            k.w(sy, "st")
            k.w(g, "st")
    k.close()
    return k.nc


def _rotary_tables(t0):
    half = np.arange(0, 64, 2, dtype=np.float64)
    inv_freq = 1.0 / 10000.0 ** (half / 64.0)
    base = (half + 0.4 * 64) / (1.4 * 64)
    base2 = np.concatenate([base, base])

    def tabs(ts, inv=False):
        fr = ts[None, :] * inv_freq[:, None]
        fr = np.concatenate([fr, fr], 0)
        sc = base2[:, None] ** (((ts - t0) / 256.0)[None, :] * (-1.0 if inv else 1.0))
        return (np.cos(fr) * sc).astype(np.float32), (np.sin(fr) * sc).astype(np.float32)

    tq = np.arange(t0, t0 + 1024, dtype=np.float64)
    tk = np.arange(t0 - 512, t0 + 1024, dtype=np.float64)
    q1, q2 = tabs(tq)
    k1, k2 = tabs(tk, inv=True)
    return 8.0 * q1, 8.0 * q2, k1, k2


def _ffn_core_consts(c):
    b, (c0, c1) = core_chunks(c)
    qt1 = np.empty((128, 2048), np.float32)
    qt2 = np.empty((128, 2048), np.float32)
    kt1 = np.empty((128, 3072), np.float32)
    kt2 = np.empty((128, 3072), np.float32)
    s01 = np.ones((128, 4), np.float32)
    for sl, w in ((0, c0), (1, c1)):
        a1, a2, b1_, b2_ = _rotary_tables(w * 1024)
        qt1[:, sl * 1024:(sl + 1) * 1024] = np.tile(a1, (2, 1))
        qt2[:, sl * 1024:(sl + 1) * 1024] = np.tile(a2, (2, 1))
        kt1[:, sl * 1536:(sl + 1) * 1536] = np.tile(b1_, (2, 1))
        kt2[:, sl * 1536:(sl + 1) * 1536] = np.tile(b2_, (2, 1))
        if w == 0:
            s01[:, sl * 2] = 0.0
    return qt1, qt2, kt1, kt2, s01


def zext_to_cores(z, halo):
    out = []
    CWh = halo + 1024
    for c in range(NC):
        b, (c0, c1) = core_chunks(c)
        buf = np.zeros((2 * CWh, D), np.float32)
        for sl, w in ((0, c0), (1, c1)):
            lo = w * 1024 - halo
            src0 = max(lo, 0)
            buf[sl * CWh + (src0 - lo): sl * CWh + CWh] = z[b, src0:w * 1024 + 1024]
        out.append(np.ascontiguousarray(buf.T))
    return out


def run_l0ffn(x, p):
    z = x
    for li in range(2):
        qsv = np.asarray(p["L0_ffn_q_scale"][li], np.float32)
        ksv = np.asarray(p["L0_ffn_k_scale"][li], np.float32)
        prm = (np.arange(64) + 32) % 64
        nc1 = build_ffn_attn(dict(outer_ln=(li == 0)))
        ins1 = []
        zts = zext_to_cores(z, 513)
        for c in range(NC):
            qt1, qt2, kt1, kt2, s01 = _ffn_core_consts(c)
            ins1.append(dict(
                zT=zts[c], wqkv=np.asarray(p["L0_ffn_wqkv"][li]),
                wo=np.asarray(p["L0_ffn_wo"][li]),
                qs=np.tile(qsv[:, None], (2, 1)), qsp=np.tile(qsv[prm][:, None], (2, 1)),
                ks=np.tile(ksv[:, None], (2, 1)), ksp=np.tile(ksv[prm][:, None], (2, 1)),
                qt1=qt1, qt2=qt2, kt1=kt1, kt2=kt2, s01i=s01))
        r1 = run_bass_kernel_spmd(nc1, ins1, core_ids=list(range(NC)))
        zmid = cores_to_x([r1.results[c]["y"] for c in range(NC)])

        outer = (li == 1)
        nc2 = build_ffn_mlp(dict(outer=outer))
        zts2 = zext_to_cores(zmid, 1)
        xts = x_to_cores(x) if outer else None
        ins2 = []
        for c in range(NC):
            d = dict(zT=zts2[c],
                     lg=np.asarray(p["L0_ffn_ln_g"][li]).reshape(8, 128).T.copy(),
                     lb=np.asarray(p["L0_ffn_ln_b"][li]).reshape(8, 128).T.copy(),
                     w1=np.asarray(p["L0_ffn_w1"][li]), b1=np.asarray(p["L0_ffn_b1"][li])[None, :],
                     w2=np.asarray(p["L0_ffn_w2"][li]), b2=np.asarray(p["L0_ffn_b2"][li])[None, :])
            if outer:
                d["xo"] = xts[c]
            ins2.append(d)
        r2 = run_bass_kernel_spmd(nc2, ins2, core_ids=list(range(NC)))
        z = cores_to_x([r2.results[c]["y"] for c in range(NC)])
    return z


def build_l0ab():
    k = K()
    nc = k.nc
    t, v, s, g, sy = nc.tensor, nc.vector, nc.scalar, nc.gpsimd, nc.sync
    xT = k.din("xT", [D, TL])
    wq = k.din("wq", [D, D])
    wk = k.din("wk", [D, D])
    wv = k.din("wv", [D, D])
    bq = k.din("bq", [1, D])
    bk = k.din("bk", [1, D])
    bv = k.din("bv", [1, D])
    wo = k.din("wo", [D, D])
    bo = k.din("bo", [1, D])
    yT = k.dout("y", [D, TL])
    SL = 512

    with k.st:
        x_sb = [k.sb(f"x{i}", [128, TL]) for i in range(8)]
        xn_sb = [k.sb(f"xn{i}", [128, TL], BF16) for i in range(8)]
        w_sb = [k.sb(f"w{i}", [128, D], BF16) for i in range(8)]
        qt_ = [k.sb(f"qt{i}", [128, D], BF16) for i in range(4)]
        kt_ = [k.sb(f"kt{i}", [128, D], BF16) for i in range(4)]
        vt_ = [k.sb(f"vt{i}", [128, D], BF16) for i in range(4)]
        ott = [k.sb(f"ot{i}", [128, D], BF16) for i in range(4)]
        ee = k.sb("ee", [128, 256])
        pp_e = k.sb("pp_e", [128, 256])
        rsum = k.sb("rsum", [128, 16])
        tmpa = k.sb("tmpa", [128, 64])
        oacc = k.sb("oacc", [128, D])
        brow = k.sb("brow", [1, D], BF16)
        borow = k.sb("borow", [1, D], BF16)
        onesrow = k.sb("onesrow", [1, TL], BF16)
        yr = k.sb("yr", [128, 512])
        ev1 = k.sb("ev1", [128, 512], BF16)
        otf = [k.sb(f"otf{i}", [128, 512], BF16) for i in range(8)]
        pp = k.ps("pp", [128, 512])
        ps_x = k.ps("ps_x", [128, 512])
        ps_t = k.ps("ps_t", [128, 128], BF16)
        scr = ln_scratch(k, TLc=TL, ps4=(ps_x[0:1, :], pp[0:1, :], ps_x, pp))
        k.sem("ld", "wt", "st", "fT", "fE", "fV", "cst", "pjt", "ev", "lnv")
        with nc.Block():
            ident = emit_ident(k)
            g.memset(onesrow[:], 1.0)
            k.sig(g, "cst")
            for i in range(8):
                k.dma(sy, x_sb[i][:], xT[i * 128:(i + 1) * 128, :], "ld")
            for e_ in (t, v, s):
                k.w(e_, "cst")
                k.w(e_, "ld")
            emit_ln(k, x_sb, xn_sb, scr)

            for slab in range(TL // SL):
                ns0 = slab * SL
                for (wap, bap, dstl, scl) in ((wq, bq, qt_, 0.125), (wk, bk, kt_, 1.0),
                                              (wv, bv, vt_, 1.0)):
                    k.w(g, "wt")
                    k.w(g, "pjt")
                    for i in range(8):
                        k.dma(g, w_sb[i][:], wap[i * 128:(i + 1) * 128, :], "wt")
                    k.dma(g, brow[:], bap, "wt")
                    k.w(t, "wt")
                    k.w(t, "lnv")
                    k.w(t, "fV")
                    for tt4 in range(4):
                        tsl = slice(ns0 + tt4 * 128, ns0 + (tt4 + 1) * 128)
                        for cb in range(2):
                            cs = slice(cb * 512, (cb + 1) * 512)
                            for kc in range(8):
                                t.matmul(pp[:], xn_sb[kc][:, tsl], w_sb[kc][:, cs],
                                         start=(kc == 0), stop=False)
                            t.matmul(pp[:], onesrow[0:1, tsl], brow[0:1, cs],
                                     start=False, stop=True)
                            k.sig(t, "pjt")
                            k.w(v, "pjt")
                            k.w(v, "st")
                            k.w(v, "fV")
                            if scl != 1.0:
                                v.tensor_scalar(dstl[tt4][:, cs], pp[:], float(scl), None, OP.mult)
                            else:
                                v.tensor_copy(dstl[tt4][:, cs], pp[:])
                            k.sig(v, "ev")
                            k.w(t, "ev")

                for tt4 in range(4):
                    for h in range(16):
                        for h2 in range(16):
                            k.w(v, "fE")
                            v.tensor_tensor(tmpa[:], qt_[tt4][:, h * 64:(h + 1) * 64],
                                            kt_[tt4][:, h2 * 64:(h2 + 1) * 64], OP.mult)
                            k.sig(v, "fV")
                            k.w(v, "fV")
                            v.tensor_reduce(ee[:, h * 16 + h2:h * 16 + h2 + 1], tmpa[:],
                                            AX.X, OP.add)
                            k.sig(v, "fV")
                    k.w(s, "fV")
                    k.inc(s.activation(pp_e[:], ee[:], AF.Exp), "fE")
                    k.w(v, "fE")
                    for h in range(16):
                        v.tensor_reduce(rsum[:, h:h + 1], pp_e[:, h * 16:(h + 1) * 16],
                                        AX.X, OP.add)
                        k.sig(v, "fV")
                    k.w(v, "fV")
                    v.reciprocal(rsum[:], rsum[:])
                    k.sig(v, "fV")
                    for h in range(16):
                        for h2 in range(16):
                            k.w(v, "fV")
                            v.tensor_scalar(tmpa[:], vt_[tt4][:, h2 * 64:(h2 + 1) * 64],
                                            pp_e[:, h * 16 + h2:h * 16 + h2 + 1], None, OP.mult)
                            k.sig(v, "fV")
                            k.w(v, "fV")
                            if h2 == 0:
                                v.tensor_copy(oacc[:, h * 64:(h + 1) * 64], tmpa[:])
                            else:
                                v.tensor_tensor(oacc[:, h * 64:(h + 1) * 64],
                                                oacc[:, h * 64:(h + 1) * 64], tmpa[:], OP.add)
                            k.sig(v, "fV")
                        k.w(v, "fV")
                        v.tensor_scalar(ott[tt4][:, h * 64:(h + 1) * 64],
                                        oacc[:, h * 64:(h + 1) * 64],
                                        rsum[:, h:h + 1], None, OP.mult)
                        k.sig(v, "fV")

                k.w(g, "wt")
                k.w(g, "pjt")
                for i in range(8):
                    k.dma(g, w_sb[i][:], wo[i * 128:(i + 1) * 128, :], "wt")
                k.dma(g, borow[:], bo, "wt")
                k.w(t, "wt")
                k.w(t, "fV")
                for tt4 in range(4):
                    for kc in range(8):
                        k.w(t, "fV")
                        t.transpose(ps_t[:], ott[tt4][:, kc * 128:(kc + 1) * 128], ident[:])
                        k.sig(t, "fT")
                        k.w(v, "fT")
                        v.tensor_copy(otf[kc][:, tt4 * 128:(tt4 + 1) * 128], ps_t[:])
                        k.sig(v, "fV")
                k.w(t, "fV")
                for ft in range(8):
                    k.w(v, "st")
                    for kc in range(8):
                        t.matmul(pp[:], w_sb[kc][:, ft * 128:(ft + 1) * 128],
                                 otf[kc][:], start=(kc == 0), stop=False)
                    t.matmul(pp[:], borow[0:1, ft * 128:(ft + 1) * 128],
                             onesrow[0:1, ns0:ns0 + SL], start=False, stop=True)
                    k.sig(t, "pjt")
                    k.w(v, "pjt")
                    v.tensor_tensor(yr[:], x_sb[ft][:, ns0:ns0 + SL], pp[:], OP.add)
                    k.sig(v, "ev")
                    k.w(t, "ev")
                    k.w(sy, "ev")
                    k.dma(sy, yT[ft * 128:(ft + 1) * 128, ns0:ns0 + SL], yr[:], "st")
            k.w(sy, "st")
            k.w(g, "st")
    k.close()
    return k.nc



CAPL = 768


def build_moe_m1():
    k = K()
    nc = k.nc
    t, v, s, g, sy = nc.tensor, nc.vector, nc.scalar, nc.gpsimd, nc.sync
    xT = k.din("xT", [D, TL])
    wgT = k.din("wgT", [D, 8])
    agb = k.din("agb", [1, 1])
    erow_o = k.dout("erowo", [1, 4096])
    rkrow_o = k.dout("rkrow", [1, 4096])
    p1o = k.dout("p1o", [128, 16])
    p2o = k.dout("p2o", [128, 16])
    cnts_o = k.dout("cnts", [8, 4])
    xnout = k.dout("xn", [TL, D], BF16)

    with k.st:
        x_sb = [k.sb(f"x{i}", [128, TL]) for i in range(8)]
        xnb = [k.sb(f"xnb{i}", [128, TL], BF16) for i in range(8)]
        wg_sb = k.sb("wg_sb", [128, 64])
        lg = k.sb("lg", [128, 8])
        lg2 = k.sb("lg2", [128, 8])
        oh1f = k.sb("oh1f", [128, 8])
        oh2f = k.sb("oh2f", [128, 8])
        tmp8 = k.sb("tmp8", [128, 8])
        sc1 = k.sb("sc1", [128, 8])
        iota8 = k.sb("iota8", [128, 8])
        iotae = k.sb("iotae", [8, 1])
        adjcol = k.sb("adjcol", [8, 1])
        ohb = k.sb("ohb", [128, 32 * 8], BF16)
        e1t = k.sb("e1t", [128, 16])
        e2t = k.sb("e2t", [128, 16])
        p1t = k.sb("p1t", [128, 16])
        p2t = k.sb("p2t", [128, 16])
        erow = k.sb("erow", [1, 4096])
        ohrow = k.sb("ohrow", [8, 4096])
        cum = k.sb("cum", [8, 4096])
        cadj = k.sb("cadj", [8, 512])
        UBIG = k.sb("UBIG", [128, 1152], BF16)
        umv = k.sb("umv", [128, 128])
        dstrow = k.sb("dstrow", [1, 4096])
        identf = k.sb("identf", [128, 128])
        idxf = k.sb("idxf", [128, 32])
        idxu = k.sb("idxu", [128, 32], U32)
        vals = k.sb("vals", [128, 64])
        rowpt = k.sb("rowpt", [128, 16])
        agbc = k.sb("agbc", [128, 1])
        zb = k.sb("zb", [128, 256])
        xns_tok = k.sb("xns_tok", [128, D], BF16)
        ps_a = k.ps("ps_a", [128, 512])
        ps_cum = k.ps("ps_cum", [128, 512])
        ps_b = k.ps("ps_b", [128, 512])
        ps_t = k.ps("ps_t", [128, 128], BF16)
        scr = ln_scratch(k, TLc=TL)
        k.sem("ld", "wt", "st", "sc", "fT", "fE", "fV", "cst", "pjt", "ev", "lnv")
        with nc.Block():
            ident = emit_ident(k)
            g.iota(iota8[:], [[1, 8]], base=0, channel_multiplier=0,
                   allow_small_or_imprecise_dtypes=True)
            g.iota(iotae[:], [[1, 1]], base=0, channel_multiplier=1,
                   allow_small_or_imprecise_dtypes=True)
            g.iota(rowpt[:], [[128, 16]], base=1, channel_multiplier=1,
                   allow_small_or_imprecise_dtypes=True)
            g.iota(umv[:], [[1, 128]], base=0, channel_multiplier=-1,
                   allow_small_or_imprecise_dtypes=True)
            g.memset(UBIG[:, 0:512], 0.0)
            g.memset(UBIG[:, 640:1152], 1.0)
            g.memset(zb[:], 0.0)
            k.sig(g, "cst")
            k.w(v, "cst")
            v.tensor_scalar(UBIG[:, 512:640], umv[:], 0.0, None, OP.is_ge)
            v.tensor_scalar(identf[:], umv[:], 0.0, None, OP.is_equal)
            v.tensor_scalar(adjcol[:], iotae[:], 0.0, None, OP.mult)
            k.bar(v, "cst")
            v.tensor_scalar(adjcol[:], adjcol[:], -1.0, None, OP.add)
            k.sig(v, "cst")
            for i in range(8):
                k.dma(sy, x_sb[i][:], xT[i * 128:(i + 1) * 128, :], "ld")
            for i in range(8):
                k.dma(g, wg_sb[:, i * 8:(i + 1) * 8], wgT[i * 128:(i + 1) * 128, :], "wt")
            k.dma(g, agbc[0:1, :], agb, "wt")
            for e_ in (t, v, s):
                k.w(e_, "cst")
                k.w(e_, "ld")
                k.w(e_, "wt")
            emit_ln(k, x_sb, x_sb, scr)
            k.w(t, "lnv")
            t.matmul(ps_b[:, 0:1], scr["ones1"][:], agbc[0:1, :], start=True, stop=True)
            k.sig(t, "fT")
            k.w(v, "fT")
            v.tensor_copy(agbc[:], ps_b[:, 0:1])
            k.bar(v, "fV")
            v.tensor_scalar(rowpt[:], rowpt[:], agbc[:, 0:1], None, OP.add)
            k.sig(v, "fV")
            for i in range(8):
                v.tensor_copy(xnb[i][:], x_sb[i][:])
            k.sig(v, "lnv")

            for i in range(16):
                tsl = slice(i * 128, (i + 1) * 128)
                for kc in range(8):
                    t.matmul(ps_a[:, 0:8], x_sb[kc][:, tsl], wg_sb[:, kc * 8:(kc + 1) * 8],
                             start=(kc == 0), stop=(kc == 7))
                k.sig(t, "pjt")
                k.w(v, "pjt")
                if i > 0:
                    k.w(v, "fV", k.c["fV"])
                v.tensor_copy(lg[:], ps_a[:, 0:8])
                k.sig(v, "ev")
                k.w(t, "ev")
                k.w(v, "ev")
                v.tensor_reduce(sc1[:, 0:1], lg[:], AX.X, OP.max)
                k.bar(v, "fV")
                v.tensor_scalar(oh1f[:], lg[:], sc1[:, 0:1], None, OP.is_equal)
                k.bar(v, "fV")
                v.tensor_copy(ohb[:, i * 8:(i + 1) * 8], oh1f[:])
                v.tensor_scalar(tmp8[:], oh1f[:], 1e30, None, OP.mult)
                k.bar(v, "fV")
                v.tensor_tensor(lg2[:], lg[:], tmp8[:], OP.subtract)
                k.bar(v, "fV")
                v.tensor_reduce(sc1[:, 1:2], lg2[:], AX.X, OP.max)
                k.bar(v, "fV")
                v.tensor_scalar(oh2f[:], lg2[:], sc1[:, 1:2], None, OP.is_equal)
                k.bar(v, "fV")
                v.tensor_copy(ohb[:, (16 + i) * 8:(17 + i) * 8], oh2f[:])
                v.tensor_tensor(tmp8[:], oh1f[:], iota8[:], OP.mult)
                k.bar(v, "fV")
                v.tensor_reduce(e1t[:, i:i + 1], tmp8[:], AX.X, OP.add)
                k.bar(v, "fV")
                v.tensor_tensor(tmp8[:], oh2f[:], iota8[:], OP.mult)
                k.bar(v, "fV")
                v.tensor_reduce(e2t[:, i:i + 1], tmp8[:], AX.X, OP.add)
                v.tensor_tensor(sc1[:, 2:3], sc1[:, 1:2], sc1[:, 0:1], OP.subtract)
                k.sig(v, "fV")
                k.w(s, "fV")
                k.inc(s.activation(sc1[:, 3:4], sc1[:, 2:3], AF.Exp), "fE")
                k.w(v, "fE")
                v.tensor_scalar(sc1[:, 4:5], sc1[:, 3:4], 1.0, None, OP.add)
                k.bar(v, "fV")
                v.reciprocal(p1t[:, i:i + 1], sc1[:, 4:5])
                k.bar(v, "fV")
                v.tensor_tensor(p2t[:, i:i + 1], sc1[:, 3:4], p1t[:, i:i + 1], OP.mult)
                k.sig(v, "fV")

            k.w(t, "fV")
            for i in range(16):
                for (src, off) in ((e1t, 0), (e2t, 2048)):
                    t.transpose(ps_b[0:1, 0:128], src[:, i:i + 1], identf[:])
                    k.sig(t, "fT")
                    k.w(v, "fT")
                    v.tensor_copy(erow[0:1, off + i * 128:off + (i + 1) * 128], ps_b[0:1, 0:128])
                    k.sig(v, "ev")
                    k.w(t, "ev")
            for blk in range(8):
                bs = slice(blk * 512, (blk + 1) * 512)
                t.matmul(ps_b[0:8, :], scr["ones1"][0:1, 0:8], erow[0:1, bs],
                         start=True, stop=True)
                k.sig(t, "fT")
                k.w(v, "fT")
                v.tensor_scalar(ohrow[:, bs], ps_b[0:8, :], iotae[:], None, OP.is_equal)
                k.sig(v, "ev")
                k.w(t, "ev")
            for blk in range(8):
                bs = slice(blk * 512, (blk + 1) * 512)
                for j in range(4):
                    ct = blk * 4 + j
                    t.matmul(ps_cum[0:8, :], ohb[:, ct * 8:(ct + 1) * 8],
                             UBIG[:, 512 - j * 128:1024 - j * 128],
                             start=(j == 0), stop=(j == 3))
                k.sig(t, "pjt")
                k.w(v, "pjt")
                if blk % 2 == 0:
                    v.tensor_copy(cum[:, bs], ps_cum[0:8, :])
                else:
                    v.tensor_scalar(cum[:, bs], ps_cum[0:8, :],
                                    cum[:, blk * 512 - 1:blk * 512], None, OP.add)
                k.sig(v, "ev")
                k.w(t, "ev")
            for blk in range(8):
                bs = slice(blk * 512, (blk + 1) * 512)
                k.w(v, "ev", k.c["ev"])
                v.tensor_scalar(cadj[:], cum[:, bs], adjcol[:], None, OP.add)
                k.bar(v, "fV")
                v.tensor_tensor(cadj[:], cadj[:], ohrow[:, bs], OP.mult)
                k.sig(v, "fV")
                k.w(t, "fV")
                t.matmul(ps_b[0:1, :], scr["ones128"][0:8, 0:1], cadj[:], start=True, stop=True)
                k.sig(t, "fT")
                k.w(v, "fT")
                v.tensor_copy(dstrow[0:1, bs], ps_b[0:1, :])
                k.sig(v, "ev")
                k.w(t, "ev")
            k.w(v, "ev", k.c["ev"])
            for sgi in range(4):
                v.tensor_copy(cadj[:, sgi:sgi + 1], cum[:, sgi * 1024 + 1023:sgi * 1024 + 1024])
            k.sig(v, "fV")
            k.w(sy, "ev")
            k.w(sy, "fV")
            k.dma(sy, erow_o, erow[:], "st")
            k.dma(sy, rkrow_o, dstrow[:], "st")
            k.dma(sy, p1o, p1t[:], "st")
            k.dma(sy, p2o, p2t[:], "st")
            k.dma(sy, cnts_o, cadj[0:8, 0:4], "st")
            k.w(t, "lnv")
            for i in range(16):
                tsl = slice(i * 128, (i + 1) * 128)
                for kc in range(8):
                    k.w(t, "ev", k.c["ev"])
                    t.transpose(ps_t[:], xnb[kc][:, tsl], ident[:])
                    k.sig(t, "fT")
                    k.w(v, "fT")
                    v.tensor_copy(xns_tok[:, kc * 128:(kc + 1) * 128], ps_t[:])
                    k.sig(v, "ev")
                k.w(sy, "ev")
                k.dma(sy, xnout[i * 128:(i + 1) * 128, :], xns_tok[:], "st")
                k.w(t, "st")
            k.w(sy, "st")
            k.w(sy, "sc")
            k.w(g, "st")
    k.close()
    return k.nc


def build_moe_m1b():
    k = K()
    nc = k.nc
    t, v, s, g, sy = nc.tensor, nc.vector, nc.scalar, nc.gpsimd, nc.sync
    CAP = 5120
    erow_i = k.din("erow", [1, 4096])
    rkrow_i = k.din("rkrow", [1, 4096])
    p1i = k.din("p1t", [128, 16])
    p2i = k.din("p2t", [128, 16])
    agb = k.din("agb", [1, 1])
    cnt_in = k.din("cnt_in", [32, 8])
    maskin = k.din("maskin", [32, 4])
    stbl = k.dout("stbl", [8 * CAP, 2])
    with k.st:
        erow = k.sb("erow_s", [1, 4096])
        rkrow = k.sb("rkrow_s", [1, 4096])
        p1t = k.sb("p1t_s", [128, 16])
        p2t = k.sb("p2t_s", [128, 16])
        cnt_sb = k.sb("cnt_sb", [32, 8])
        msk_sb = k.sb("msk_sb", [32, 4])
        base = k.sb("base", [8, 4])
        iotae = k.sb("iotae", [8, 1])
        ecap = k.sb("ecap", [8, 1])
        ohrow = k.sb("ohrow", [8, 512])
        cadj = k.sb("cadj", [8, 512])
        brow = k.sb("brow", [1, 4096])
        grow = k.sb("grow", [1, 512])
        vldr = k.sb("vldr", [1, 512])
        dst2 = k.sb("dst2", [1, 4096])
        ones1 = k.sb("ones1", [1, 128])
        o8 = k.sb("o8", [8, 1])
        rowpt = k.sb("rowpt", [128, 16])
        agbc = k.sb("agbc", [128, 1])
        idxf = k.sb("idxf", [128, 32])
        idxu = k.sb("idxu", [128, 32], U32)
        vals = k.sb("vals", [128, 64])
        zb = k.sb("zb", [128, 640])
        ps_b = k.ps("ps_b", [128, 512])
        ps_c = k.ps("ps_c", [128, 512])
        k.sem("ld", "wt", "st", "sc", "fT", "fV", "cst", "ev")
        with nc.Block():
            g.iota(iotae[:], [[1, 1]], base=0, channel_multiplier=1,
                   allow_small_or_imprecise_dtypes=True)
            g.iota(rowpt[:], [[128, 16]], base=1, channel_multiplier=1,
                   allow_small_or_imprecise_dtypes=True)
            g.memset(ones1[:], 1.0)
            g.memset(o8[:], 1.0)
            g.memset(zb[:], 0.0)
            k.sig(g, "cst")
            k.dma(g, stbl, zb[:], "sc")
            k.dma(sy, erow[:], erow_i, "ld")
            k.dma(sy, rkrow[:], rkrow_i, "ld")
            k.dma(sy, p1t[:], p1i, "ld")
            k.dma(sy, p2t[:], p2i, "ld")
            k.dma(sy, cnt_sb[:], cnt_in, "ld")
            k.dma(sy, msk_sb[:], maskin, "ld")
            k.dma(sy, agbc[0:1, :], agb, "ld")
            for e_ in (t, v):
                k.w(e_, "cst")
                k.w(e_, "ld")
            t.matmul(ps_b[0:8, 0:4], cnt_sb[:], msk_sb[:], start=True, stop=True)
            t.matmul(ps_c[:, 0:1], ones1[:], agbc[0:1, :], start=True, stop=True)
            k.sig(t, "fT")
            k.w(v, "fT")
            v.tensor_copy(base[:], ps_b[0:8, 0:4])
            v.tensor_copy(agbc[:], ps_c[:, 0:1])
            k.bar(v, "fV")
            v.tensor_scalar(rowpt[:], rowpt[:], agbc[:, 0:1], None, OP.add)
            v.tensor_scalar(ecap[:], iotae[:], 5120.0, None, OP.mult)
            k.sig(v, "fV")
            k.w(t, "fV")
            for blk in range(8):
                bs = slice(blk * 512, (blk + 1) * 512)
                sgi = blk // 2
                t.matmul(ps_b[0:8, :], ones1[0:1, 0:8], erow[0:1, bs], start=True, stop=True)
                k.sig(t, "fT")
                k.w(v, "fT")
                v.tensor_scalar(ohrow[:], ps_b[0:8, :], iotae[:], None, OP.is_equal)
                k.bar(v, "ev")
                v.tensor_scalar(cadj[:], ohrow[:], base[:, sgi:sgi + 1], None, OP.mult)
                k.sig(v, "fV")
                k.w(t, "fV")
                t.matmul(ps_c[0:1, :], o8[:], cadj[:], start=True, stop=True)
                k.sig(t, "fT")
                k.w(v, "fT")
                v.tensor_tensor(grow[:], ps_c[0:1, :], rkrow[0:1, bs], OP.add)
                k.bar(v, "ev")
                v.tensor_scalar(vldr[:], grow[:], 5120.0, None, OP.is_ge)
                k.bar(v, "ev")
                v.tensor_scalar(vldr[:], vldr[:], 1e6, None, OP.mult)
                k.bar(v, "ev")
                v.tensor_tensor(grow[:], grow[:], vldr[:], OP.add)
                k.bar(v, "ev")
                k.w(v, "fV", k.c["fV"])
                v.tensor_scalar(cadj[:], ohrow[:], ecap[:, 0:1], None, OP.mult)
                k.sig(v, "fV")
                k.w(t, "fV")
                t.matmul(ps_c[0:1, :], o8[:], cadj[:], start=True, stop=True)
                k.sig(t, "fT")
                k.w(v, "fT")
                v.tensor_tensor(dst2[0:1, bs], grow[:], ps_c[0:1, :], OP.add)
                k.sig(v, "ev")
                k.w(t, "ev")
            k.w(sy, "ev")
            for m in range(32):
                k.dma(sy, idxf[:, m:m + 1], dst2[0:1, m * 128:(m + 1) * 128], "st")
            k.w(v, "st")
            v.tensor_copy(idxu[:], idxf[:])
            k.bar(v, "fV")
            for i in range(16):
                for kk, pt in ((0, p1t), (1, p2t)):
                    m = kk * 16 + i
                    v.tensor_copy(vals[:, 2 * m:2 * m + 1], rowpt[:, i:i + 1])
                    v.tensor_copy(vals[:, 2 * m + 1:2 * m + 2], pt[:, i:i + 1])
            k.sig(v, "fV")
            k.w(g, "fV")
            k.w(g, "sc")
            for m in range(32):
                k.inc(g.indirect_dma_start(stbl, IndirectOffsetOnAxis(ap=idxu[:, m:m + 1], axis=0),
                                           vals[:, 2 * m:2 * m + 2], None,
                                           bounds_check=8 * 5120 - 1, oob_is_err=False), "sc", 16)
            k.w(sy, "sc")
            k.w(g, "st")
    k.close()
    return k.nc


def build_moe_m2(capl=256):
    k = K()
    nc = k.nc
    t, v, s, g, sy = nc.tensor, nc.vector, nc.scalar, nc.gpsimd, nc.sync
    NT = 8 * capl
    NTI = NT // 128
    HF = 4096
    stbls_i = [k.din(f"stbl{r}", [NT, 2]) for r in range(8)]
    xnall = k.din("xnall", [T * 2, D], BF16)
    cfc = k.din("cfc", [D, HF])
    cpj = k.din("cpj", [HF, D])
    part = k.dout("part", [T * 2, D])

    with k.st:
        cfc_sb = k.sb("cfc_sb", [128, 8 * HF], BF16)
        cpj_sb = k.sb("cpj_sb", [128, 32 * D], BF16)
        su = k.sb("su", [128, NTI * 2])
        su8 = k.sb("su8", [128, 8 * NTI * 2])
        idxf = k.sb("idxf", [128, NTI])
        msk_ = k.sb("msk_", [128, NTI])
        idxu = k.sb("idxu", [128, NTI], U32)
        xg = k.sb("xg", [128, D], BF16)
        xgT = k.sb("xgT", [128, D], BF16)
        hrelu = k.sb("hrelu", [128, 128])
        hT = k.sb("hT", [128, HF], BF16)
        ow = k.sb("ow", [128, D])
        zb = k.sb("zb", [128, 2048])
        ps_t = k.ps("ps_t", [128, 128], BF16)
        ps_fa = k.ps("ps_fa", [128, 512])
        ps_fb = k.ps("ps_fb", [128, 512])
        ps_p0 = k.ps("ps_p0", [128, 512])
        ps_p1 = k.ps("ps_p1", [128, 512])
        k.sem("ld", "wt", "st", "sc", "gt", "fT", "fE", "fV", "cst", "pjt", "ev")
        with nc.Block():
            ident = emit_ident(k)
            g.memset(zb[:], 0.0)
            k.sig(g, "cst")
            k.w(g, "cst")
            for i in range(64):
                k.dma(g, part[i * 256:(i + 1) * 256, :], zb[:], "sc")
            for i in range(8):
                k.dma(g, cfc_sb[:, i * HF:(i + 1) * HF], cfc[i * 128:(i + 1) * 128, :], "ld")
            for i in range(32):
                k.dma(g, cpj_sb[:, i * D:(i + 1) * D], cpj[i * 128:(i + 1) * 128, :], "ld")
            for r in range(8):
                for m in range(NTI):
                    k.dma(g, su8[:, (r * NTI + m) * 2:(r * NTI + m) * 2 + 2],
                          stbls_i[r][m * 128:(m + 1) * 128, :], "wt")
            k.w(v, "wt")
            v.tensor_tensor(su[:], su8[:, 0:NTI * 2], su8[:, NTI * 2:2 * NTI * 2], OP.add)
            for r in range(2, 8):
                k.bar(v, "ev")
                v.tensor_tensor(su[:], su[:], su8[:, r * NTI * 2:(r + 1) * NTI * 2], OP.add)
            k.bar(v, "ev")
            v.tensor_scalar(idxf[:], su[:, 0:2 * NTI:2], -1.0, None, OP.add)
            k.bar(v, "fV")
            v.tensor_scalar(msk_[:], idxf[:], 0.0, None, OP.is_lt)
            k.bar(v, "fV")
            v.tensor_scalar(msk_[:], msk_[:], 1e6, None, OP.mult)
            k.bar(v, "fV")
            v.tensor_tensor(idxf[:], idxf[:], msk_[:], OP.add)
            k.bar(v, "fV")
            v.tensor_copy(idxu[:], idxf[:])
            k.sig(v, "fV")
            k.w(g, "fV")
            k.w(g, "sc")
            k.w(t, "ld")
            k.w(s, "cst")
            for m in range(NTI):
                k.w(g, "ev", k.c["ev"])
                k.inc(g.indirect_dma_start(xg[:], None, xnall,
                                           IndirectOffsetOnAxis(ap=idxu[:, m:m + 1], axis=0),
                                           bounds_check=16383, oob_is_err=False), "gt", 16)
                k.w(t, "gt")
                for kc in range(8):
                    t.transpose(ps_t[:], xg[:, kc * 128:(kc + 1) * 128], ident[:])
                    k.sig(t, "fT")
                    k.w(v, "fT")
                    v.tensor_copy(xgT[:, kc * 128:(kc + 1) * 128], ps_t[:])
                    k.sig(v, "ev")
                    k.w(t, "ev")
                for hc in range(32):
                    pf = ps_fa if (hc % 2 == 0) else ps_fb
                    if hc >= 2:
                        k.w(t, "fV", k.c["fV"])
                    for fc in range(8):
                        t.matmul(pf[:, 0:128], cfc_sb[:, fc * HF + hc * 128:fc * HF + (hc + 1) * 128],
                                 xgT[:, fc * 128:(fc + 1) * 128],
                                 start=(fc == 0), stop=(fc == 7))
                    k.sig(t, "pjt")
                    k.w(v, "pjt")
                    v.tensor_scalar(hrelu[:], pf[:, 0:128], 0.0, None, OP.max)
                    k.bar(v, "ev")
                    v.tensor_tensor(hT[:, hc * 128:(hc + 1) * 128], hrelu[:], hrelu[:], OP.mult)
                    k.sig(v, "fV")
                k.w(t, "fV")
                k.w(v, "sc", k.c["sc"])
                for cb in range(2):
                    pp_ = ps_p0 if cb == 0 else ps_p1
                    for hc in range(32):
                        t.matmul(pp_[:], hT[:, hc * 128:(hc + 1) * 128],
                                 cpj_sb[:, hc * D + cb * 512:hc * D + (cb + 1) * 512],
                                 start=(hc == 0), stop=(hc == 31))
                    k.sig(t, "pjt")
                    k.w(v, "pjt")
                    v.tensor_scalar(ow[:, cb * 512:(cb + 1) * 512], pp_[:],
                                    su[:, m * 2 + 1:m * 2 + 2], None, OP.mult)
                    k.sig(v, "fV")
                k.w(g, "fV")
                k.inc(g.indirect_dma_start(part, IndirectOffsetOnAxis(ap=idxu[:, m:m + 1], axis=0),
                                           ow[:], None,
                                           bounds_check=16383, oob_is_err=False), "sc", 16)
                k.w(t, "fV")
            k.w(sy, "sc")
            k.w(g, "sc")
    k.close()
    return k.nc


def build_moe_m3(npp=24):
    k = K()
    nc = k.nc
    t, v, s, g, sy = nc.tensor, nc.vector, nc.scalar, nc.gpsimd, nc.sync
    xT = k.din("xT", [D, TL])
    pps = [k.din(f"pp{r}", [TL, D]) for r in range(npp)]
    yT = k.dout("y", [D, TL])
    with k.st:
        x_sb = [k.sb(f"x{i}", [128, TL]) for i in range(8)]
        lda = k.sb("lda", [128, npp * D])
        acc = k.sb("acc", [128, D])
        y_sb = k.sb("y_sb", [128, 128])
        identf = k.sb("identf", [128, 128])
        iv = k.sb("iv", [128, 128])
        ps_tf = k.ps("ps_tf", [128, 128])
        k.sem("ld", "wt", "st", "fT", "fV", "cst", "ev")
        with nc.Block():
            g.iota(iv[:], [[1, 128]], base=0, channel_multiplier=-1,
                   allow_small_or_imprecise_dtypes=True)
            k.sig(g, "cst")
            k.w(v, "cst")
            v.tensor_scalar(identf[:], iv[:], 0.0, None, OP.is_equal)
            k.sig(v, "cst")
            for i in range(8):
                k.dma(sy, x_sb[i][:], xT[i * 128:(i + 1) * 128, :], "ld")
            k.w(t, "cst")
            k.w(v, "ld")
            for i in range(16):
                tsl = slice(i * 128, (i + 1) * 128)
                if i > 0:
                    k.w(g, "fV", k.c["fV"])
                for r in range(npp):
                    k.dma(g, lda[:, r * D:(r + 1) * D], pps[r][tsl, :], "wt")
                k.w(v, "wt")
                v.tensor_tensor(acc[:], lda[:, 0:D], lda[:, D:2 * D], OP.add)
                for r in range(2, npp):
                    k.bar(v, "ev")
                    v.tensor_tensor(acc[:], acc[:], lda[:, r * D:(r + 1) * D], OP.add)
                k.sig(v, "fV")
                k.w(t, "fV")
                for kc in range(8):
                    if not (i == 0 and kc == 0):
                        k.w(t, "st", k.c["st"])
                    t.transpose(ps_tf[:], acc[:, kc * 128:(kc + 1) * 128], identf[:])
                    k.sig(t, "fT")
                    k.w(v, "fT")
                    v.tensor_tensor(y_sb[:], ps_tf[:], x_sb[kc][:, tsl], OP.add)
                    k.sig(v, "ev")
                    k.w(t, "ev")
                    k.w(sy, "ev")
                    k.dma(sy, yT[kc * 128:(kc + 1) * 128, tsl], y_sb[:], "st")
            k.w(sy, "st")
            k.w(g, "st")
    k.close()
    return k.nc


def run_moe(x, w_g, c_fc, c_proj):
    CAP = 5120
    xts = x_to_cores(x)
    wgT = np.ascontiguousarray(w_g.T.astype(np.float32))
    ncM1 = build_moe_m1()
    r1 = run_bass_kernel_spmd(ncM1, [
        {"xT": xts[c], "wgT": wgT, "agb": np.array([[c * 2048.0]], np.float32)}
        for c in range(8)], core_ids=list(range(8)))
    xnall = np.concatenate([np.asarray(r1.results[c]["xn"]) for c in range(8)], 0)
    cnts = [np.asarray(r1.results[c]["cnts"]) for c in range(8)]
    cnt_in = np.concatenate([cnts[r].T for r in range(8)], 0).astype(np.float32)
    cnt_in = np.ascontiguousarray(cnt_in)

    def chunk_pos(r, sgi):
        b, q = divmod(r, 4)
        ch = (q if sgi % 2 == 0 else 7 - q)
        return 8 * b + ch

    ncM1b = build_moe_m1b()
    maps_b = []
    for c in range(8):
        mk = np.zeros((32, 4), np.float32)
        for scol in range(4):
            kcol = scol // 2
            mypos = chunk_pos(c, scol % 2)
            for r in range(8):
                for s in range(4):
                    ks = s // 2
                    pos = chunk_pos(r, s % 2)
                    if kcol == 1 and ks == 0:
                        mk[r * 4 + s, scol] = 1.0
                    elif ks == kcol and pos < mypos:
                        mk[r * 4 + s, scol] = 1.0
        maps_b.append({"erow": np.asarray(r1.results[c]["erowo"]),
                       "rkrow": np.asarray(r1.results[c]["rkrow"]),
                       "p1t": np.asarray(r1.results[c]["p1o"]),
                       "p2t": np.asarray(r1.results[c]["p2o"]),
                       "agb": np.array([[c * 2048.0]], np.float32),
                       "cnt_in": cnt_in, "maskin": mk})
    r1b = run_bass_kernel_spmd(ncM1b, maps_b, core_ids=list(range(8)))
    stbls = [np.asarray(r1b.results[c]["stbl"]) for c in range(8)]

    ncM2 = build_moe_m2(320)
    parts = []
    for q in range(2):
        maps = []
        for e in range(8):
            mp = {"xnall": xnall, "cfc": c_fc[e], "cpj": c_proj[e]}
            for r in range(8):
                mp[f"stbl{r}"] = np.ascontiguousarray(
                    stbls[r][e * CAP + q * 2560:e * CAP + (q + 1) * 2560])
            maps.append(mp)
        r2 = run_bass_kernel_spmd(ncM2, maps, core_ids=list(range(8)))
        parts += [np.asarray(r2.results[e]["part"]) for e in range(8)]
    ncM3 = build_moe_m3(16)
    maps3 = []
    for c in range(8):
        mp = {"xT": xts[c]}
        for r in range(16):
            mp[f"pp{r}"] = np.ascontiguousarray(parts[r][c * 2048:(c + 1) * 2048])
        maps3.append(mp)
    r3 = run_bass_kernel_spmd(ncM3, maps3, core_ids=list(range(8)))
    return cores_to_x([np.asarray(r3.results[c]["y"]) for c in range(8)])


def run_l0ab(x, p):
    xts = x_to_cores(x)
    nc = build_l0ab()
    ins = [dict(xT=xts[c], wq=p["L0_attn_b_Wq"], wk=p["L0_attn_b_Wk"],
                wv=p["L0_attn_b_Wv"], bq=p["L0_attn_b_bq"][None, :],
                bk=p["L0_attn_b_bk"][None, :], bv=p["L0_attn_b_bv"][None, :],
                wo=p["L0_attn_b_Wo"], bo=p["L0_attn_b_bo"][None, :]) for c in range(NC)]
    r = run_bass_kernel_spmd(nc, ins, core_ids=list(range(NC)))
    return cores_to_x([np.asarray(r.results[c]["y"]) for c in range(NC)])


def kernel(**inputs):
    p = {k2: np.asarray(v) for k2, v in inputs.items()}
    x = p["L0_attn_a_x"].astype(np.float32)

    cfgA = dict(NQ=512, NV=512, bias=True, qscale=1.0 / np.sqrt(32.0), fold="l0aa")
    cfgB = dict(dk=32, NV=512, causal=False, bias=True, fold="l0aa")
    x = run_attn_sublayer(x, cfgA, cfgB, p["L0_attn_a_Wq"], p["L0_attn_a_Wk"],
                          p["L0_attn_a_Wv"], p["L0_attn_a_bq"], p["L0_attn_a_bk"],
                          p["L0_attn_a_bv"], np.ascontiguousarray(p["L0_attn_a_Wo"].T),
                          p["L0_attn_a_bo"], extraA={"Vm": p["L0_attn_a_V"]},
                          extraB={"Um": p["L0_attn_a_U"]})
    x = run_l0ffn(x, p)
    x = run_l0ab(x, p)
    x = run_l0ffn(x, p)

    Wsh = p["L1_attn_a_W_share"]; bsh = p["L1_attn_a_b_share"]
    Wq1 = p["L1_attn_a_Wq"]; bq1 = p["L1_attn_a_bq"]
    Wk1 = p["L1_attn_a_Wk"]; bk1 = p["L1_attn_a_bk"]
    wqf = np.empty((1024, 1024), np.float32)
    wkf = np.empty((1024, 1024), np.float32)
    bqf = np.empty(1024, np.float32)
    bkf = np.empty(1024, np.float32)
    for h in range(16):
        wqf[:, h * 64:h * 64 + 16] = Wsh
        wqf[:, h * 64 + 16:(h + 1) * 64] = Wq1[:, h * 48:(h + 1) * 48]
        wkf[:, h * 64:h * 64 + 16] = Wsh
        wkf[:, h * 64 + 16:(h + 1) * 64] = Wk1[:, h * 48:(h + 1) * 48]
        bqf[h * 64:h * 64 + 16] = bsh
        bqf[h * 64 + 16:(h + 1) * 64] = bq1[h * 48:(h + 1) * 48]
        bkf[h * 64:h * 64 + 16] = bsh
        bkf[h * 64 + 16:(h + 1) * 64] = bk1[h * 48:(h + 1) * 48]
    cfgA1 = dict(NQ=1024, NV=1024, bias=True, qscale=1.0 / 8.0)
    cfgB1 = dict(dk=64, NV=1024, causal=True, bias=True)
    x = run_attn_sublayer(x, cfgA1, cfgB1, wqf, wkf, p["L1_attn_a_Wv"], bqf, bkf,
                          p["L1_attn_a_bv"], p["L1_attn_a_Wo"], p["L1_attn_a_bo"])
    x = run_moe(x, p["L1_ffn_w_g"], p["L1_ffn_c_fc"], p["L1_ffn_c_proj"])
    WQ = p["L1_attn_b_WQ"]; WK = p["L1_attn_b_WK"]
    WV = p["L1_attn_b_WV"]; WO = p["L1_attn_b_WO"]
    wq = np.ascontiguousarray(WQ.transpose(2, 0, 1).reshape(1024, 1024))
    wk = np.ascontiguousarray(WK.transpose(2, 0, 1).reshape(1024, 1024))
    wv = np.ascontiguousarray(WV.transpose(2, 0, 1).reshape(1024, 1024))
    wot = np.ascontiguousarray(WO.transpose(0, 2, 1).reshape(1024, 1024))
    cfgA2 = dict(NQ=1024, NV=1024, bias=False, qscale=1.0 / 8.0)
    cfgB2 = dict(dk=64, NV=1024, causal=True, bias=False)
    x = run_attn_sublayer(x, cfgA2, cfgB2, wq, wk, wv, None, None, None, wot, None)
    x = run_moe(x, p["L1_ffn_w_g"], p["L1_ffn_c_fc"], p["L1_ffn_c_proj"])
    return x.astype(np.float32)
